# version 03 (final)
# baseline (speedup 1.0000x reference)
import numpy as np
import ml_dtypes
import concourse.bass as bass
import concourse.mybir as mybir
from concourse.bass_utils import run_bass_kernel_spmd
from contextlib import ExitStack

F32 = mybir.dt.float32
BF16 = mybir.dt.bfloat16
AF = mybir.ActivationFunctionType
ALU = mybir.AluOpType
AX = mybir.AxisListType.X

D, T, DI, DS = 1024, 2048, 2048, 16
NH, HS, FQ = 16, 64, 256
HID, NE = 2048, 14
POSW = 2560
LN_EPS = 1e-5
RMS_EPS = 1.1920929e-07
NEG = -1e9
WSC = float((3.0 * D * HS * NH) ** (-0.25))
HSC = float(HS ** (-0.25))
OKSC = float((D * D) ** (-0.25))

_STAGE = "full"
_PH4 = 4
_PH6 = 4

CONSTS = [LN_EPS, RMS_EPS, WSC * HSC, WSC, HSC, 1e-9, NEG, OKSC, 0.5, -0.5,
          2.0, 1.0, -1.0, 0.0, 1.0 / 1024.0, 1.0 / 2048.0]

KB = 1024
A_XNM = 0 * KB
A_XN = 32 * KB
A_XRES = 66 * KB
A_SQ = 130 * KB
A_TMP1 = 140 * KB
A_XK = 34 * KB
A_KACT = 66 * KB
A_WT = 98 * KB
A_WT2 = 114 * KB
A_XNS = 130 * KB
A_TMP2 = 140 * KB
A_XB = 32 * KB
A_WINX = 96 * KB
A_XBR = 128 * KB
A_CT = 138 * KB
A_DELTA = 96 * KB
A_DXB = 112 * KB
A_Y = 120 * KB
A_STRM = 136 * KB
A_SC = 144 * KB
A_XRES1 = 34 * KB
A_SQ1 = 100 * KB
A_TMP5 = 110 * KB
A_OH = 32 * KB
A_QP = 96 * KB
A_KP = 100 * KB
A_VP = 104 * KB
A_EMB = 108 * KB
A_REGA = 113 * KB
A_P = 118 * KB
A_SF = 122 * KB
A_REL = 126 * KB
A_WS6 = 128 * KB
A_SP6 = 144 * KB
A_XHAT = 34 * KB
A_LOG = 66 * KB
A_TMP7 = 76 * KB
A_XRES8 = 66 * KB
A_TR8 = 132 * KB
A_S1 = 32 * KB
A_W1 = 48 * KB
A_W2 = 112 * KB
A_TMP9 = 144 * KB

RB_RS = 0
RB_MU = 2048
RB_SC = 4096
RB_Q = 4608

SM_LNG, SM_LNB, SM_MUR, SM_MUK = 0, 8, 16, 24
SM_CB, SM_BD, SM_DP, SM_OMR, SM_OMK = 32, 48, 64, 80, 88
SM_QB, SM_OB, SM_RMS, SM_RMSS = 96, 112, 120, 232


def _sinusoid(seq_len, freqs, max_period):
    pos = np.arange(seq_len, dtype=np.float64)[:, None]
    idx = np.arange(freqs)[None, :]
    inv_freq = max_period ** (-(idx // 2 * 2) / freqs)
    angle = pos * inv_freq
    return np.where(idx % 2 == 0, np.sin(angle), np.cos(angle)).astype(np.float32)


def host_constants():
    c = {}
    sins = (np.sqrt(2.0) * _sinusoid(T, FQ, 10000).astype(np.float64)
            * (FQ ** (-0.5)) * (HS ** (-0.25))).astype(np.float32)
    cs = np.zeros((FQ, POSW), np.float32)
    cs[:, 0:T] = sins[::-1, :].T
    c["c_sins"] = cs.astype(ml_dtypes.bfloat16)
    c["c_ident_b"] = np.eye(128, dtype=ml_dtypes.bfloat16)
    c["c_ident_f"] = np.eye(128, dtype=np.float32)
    c["c_ones_b"] = np.ones((128, 128), dtype=ml_dtypes.bfloat16)
    c["c_ones_f"] = np.ones((128, 128), dtype=np.float32)
    jj = np.arange(512)[None, None, :]
    pp = np.arange(128)[:, None, None]
    vv = np.arange(4)[None, :, None]
    m = np.where(jj <= vv * 128 + pp, 0.0, NEG).astype(np.float32)
    m = np.broadcast_to(m, (128, 4, 512))
    c["c_mask"] = np.ascontiguousarray(m.reshape(128, 2048)).astype(ml_dtypes.bfloat16)
    c["c_pool"] = np.broadcast_to(np.array(CONSTS, np.float32)[None, :],
                                  (128, len(CONSTS))).copy()
    sel = np.zeros((DS, DS * 128), np.float32)
    for s in range(DS):
        sel[s, s * 128:(s + 1) * 128] = 1.0
    c["c_sel"] = sel.astype(ml_dtypes.bfloat16)
    return c


def stage_inputs(inputs, core):
    m = {}
    m["x_t"] = np.ascontiguousarray(inputs["L0_attn_x"][core].T).astype(np.float32)
    m["w_in"] = inputs["L0_attn_W_in"]
    m["conv_w"] = np.ascontiguousarray(inputs["L0_attn_conv_w"].reshape(DI, 4))
    m["conv_b"] = inputs["L0_attn_conv_b"]
    m["w_b"] = inputs["L0_attn_W_b"]
    m["w_c"] = inputs["L0_attn_W_c"]
    m["w_delta"] = inputs["L0_attn_W_delta"]
    m["b_delta"] = inputs["L0_attn_b_delta"]
    m["log_a"] = inputs["L0_attn_log_A"]
    m["dp"] = inputs["L0_attn_Dp"]
    m["w_out"] = inputs["L0_attn_W_out"]
    m["ln_g"] = inputs["L0_attn_ln_g"]
    m["ln_b"] = inputs["L0_attn_ln_b"]
    m["w_r"] = inputs["L0_ffn_W_r"]
    m["w_k"] = inputs["L0_ffn_W_k"]
    m["w_v"] = inputs["L0_ffn_W_v"]
    m["mu_r"] = inputs["L0_ffn_mu_r"]
    m["mu_k"] = inputs["L0_ffn_mu_k"]
    m["qkv_f"] = np.ascontiguousarray(inputs["L1_attn_qkv"].reshape(D, 3 * NH * HS))
    m["qb_dup"] = np.ascontiguousarray(
        np.concatenate([inputs["L1_attn_q_bias"].T, inputs["L1_attn_q_bias"].T], axis=0))
    m["pos_f"] = np.ascontiguousarray(inputs["L1_attn_positional"].reshape(FQ, NH * HS))
    m["out_k"] = inputs["L1_attn_out_kernel"]
    m["out_b"] = inputs["L1_attn_out_bias"]
    m["router"] = inputs["L1_ffn_router_w"]
    m["rms_t"] = np.ascontiguousarray(inputs["L1_ffn_rms_w"].T)
    m["w1"] = np.ascontiguousarray(inputs["L1_ffn_w1"].reshape(NE * D, HID))
    m["w2"] = np.ascontiguousarray(inputs["L1_ffn_w2"].reshape(NE * HID, D))
    m["rms_s_t"] = np.ascontiguousarray(inputs["L1_ffn_rms_w_s"].T)
    m["w1s"] = np.ascontiguousarray(inputs["L1_ffn_w1_s"].reshape(2 * D, HID))
    m["w2s"] = np.ascontiguousarray(inputs["L1_ffn_w2_s"].reshape(2 * HID, D))
    m.update(host_constants())
    return m


def build(stage):
    nc = bass.Bass("TRN2", target_bir_lowering=False)
    stk = ExitStack()

    def dram(name, shape, dtype=F32, kind="ExternalInput"):
        return nc.dram_tensor(name, shape, dtype, kind=kind).ap()

    din = {}
    f32_in = [("x_t", [D, T]), ("w_in", [D, 2 * DI]), ("conv_w", [DI, 4]),
              ("conv_b", [DI]), ("w_b", [DI, DS]), ("w_c", [DI, DS]),
              ("w_delta", [DI, DI]), ("b_delta", [DI]), ("log_a", [DI, DS]),
              ("dp", [DI]), ("w_out", [DI, D]), ("ln_g", [D]), ("ln_b", [D]),
              ("w_r", [D, D]), ("w_k", [D, D]), ("w_v", [D, D]),
              ("mu_r", [D]), ("mu_k", [D]), ("qkv_f", [D, 3 * NH * HS]),
              ("qb_dup", [128, NH]), ("pos_f", [FQ, NH * HS]),
              ("out_k", [D, D]), ("out_b", [D]), ("router", [D, NE]),
              ("rms_t", [D, NE]), ("w1", [NE * D, HID]), ("w2", [NE * HID, D]),
              ("rms_s_t", [D, 2]), ("w1s", [2 * D, HID]), ("w2s", [2 * HID, D]),
              ("c_ident_f", [128, 128]), ("c_ones_f", [128, 128]),
              ("c_pool", [128, 16])]
    for nm, sh in f32_in:
        din[nm] = dram(nm, sh)
    for nm, sh in [("c_sins", [FQ, POSW]), ("c_ident_b", [128, 128]),
                   ("c_ones_b", [128, 128]), ("c_sel", [DS, DS * 128]),
                   ("c_mask", [128, 2048])]:
        din[nm] = dram(nm, sh, BF16)

    out_t = dram("out_t", [T, D], kind="ExternalOutput")
    dbg = dram("dbg", [128, 40960], kind="ExternalOutput")

    scr_xres = dram("scr_xres", [D, T], kind="Internal")
    scr_xn = dram("scr_xn", [D, T], BF16, kind="Internal")
    scr_mo = dram("scr_mo", [D, T], kind="Internal")
    scr_pos = dram("scr_pos", [T, POSW], BF16, kind="Internal")
    scr_wdelta = dram("scr_wdelta", [DI, DI], BF16, kind="Internal")
    scr_winz = dram("scr_winz", [D, DI], BF16, kind="Internal")
    scr_wout = dram("scr_wout", [DI, D], BF16, kind="Internal")
    scr_wr = dram("scr_wr", [D, D], BF16, kind="Internal")
    scr_wk = dram("scr_wk", [D, D], BF16, kind="Internal")
    scr_wv = dram("scr_wv", [D, D], BF16, kind="Internal")

    ident_b = stk.enter_context(nc.sbuf_tensor("ident_b", [128, 128], BF16))
    ident_f = stk.enter_context(nc.sbuf_tensor("ident_f", [128, 128], F32))
    ones_b = stk.enter_context(nc.sbuf_tensor("ones_b", [128, 128], BF16))
    ones_f = stk.enter_context(nc.sbuf_tensor("ones_f", [128, 128], F32))
    sel_t = stk.enter_context(nc.sbuf_tensor("sel_t", [DS, DS * 128], BF16))
    mask_t = stk.enter_context(nc.sbuf_tensor("mask_t", [128, 2048], BF16))
    sins0 = stk.enter_context(nc.sbuf_tensor("sins0", [128, POSW], BF16))
    sins1 = stk.enter_context(nc.sbuf_tensor("sins1", [128, POSW], BF16))
    smalls = stk.enter_context(nc.sbuf_tensor("smalls", [128, 256], F32))
    convw_t = stk.enter_context(nc.sbuf_tensor("convw_t", [128, 64], F32))
    aneg_t = stk.enter_context(nc.sbuf_tensor("aneg_t", [128, 256], F32))
    hcar_t = stk.enter_context(nc.sbuf_tensor("hcar_t", [128, 256], F32))
    wb_t = stk.enter_context(nc.sbuf_tensor("wb_t", [128, 256], BF16))
    wc_t = stk.enter_context(nc.sbuf_tensor("wc_t", [128, 256], BF16))
    bmcm = stk.enter_context(nc.sbuf_tensor("bmcm", [16, 4096], BF16))
    rowbuf = stk.enter_context(nc.sbuf_tensor("rowbuf", [1, 4800], F32))
    getok = stk.enter_context(nc.sbuf_tensor("getok", [128, 16 * NE], F32))
    cpool = stk.enter_context(nc.sbuf_tensor("cpool", [128, 16], F32))
    arena = stk.enter_context(nc.sbuf_tensor("arena", [128, 76 * KB], BF16))
    AB = arena[:]
    AFp = arena[:].bitcast(F32)

    def bf(off, cols):
        assert off % 2 == 0
        return AB[:, off // 2: off // 2 + cols]

    def f32(off, cols):
        assert off % 4 == 0
        return AFp[:, off // 4: off // 4 + cols]

    PSUM = stk.enter_context(nc.psum_tensor("ps", [128, 4096], F32))

    def bank(b, cols=512, row0=0, rows=128, col0=0):
        return PSUM[row0:row0 + rows, b * 512 + col0:b * 512 + col0 + cols]

    class Sem:
        def __init__(self, name):
            self.h = stk.enter_context(nc.semaphore(name))
            self.n = 0

        def inc(self, instr, k=1):
            instr.then_inc(self.h, k)
            self.n += k

        def wait(self, eng, tok=None):
            eng.wait_ge(self.h, self.n if tok is None else tok)

    sD = Sem("sD")
    sG = Sem("sG")
    sT = Sem("sT")
    sV = Sem("sV")
    sS = Sem("sS")
    sP = Sem("sP")
    sW = Sem("sW")
    SEMS = [sD, sG, sT, sV, sS, sP, sW]

    for ci, cv in enumerate(CONSTS):
        nc.const_aps.register(F32, float(cv), cpool[:, ci:ci + 1])

    blk = stk.enter_context(nc.Block())

    def barrier():
        toks = [(s, s.n) for s in SEMS]
        for en in ("sync", "gpsimd", "tensor", "vector", "scalar"):
            def body(e, toks=toks):
                for s, t in toks:
                    if t > 0:
                        e.wait_ge(s.h, t)
            getattr(blk, en)(body)

    def fm_ap(dram_ap, ntiles, width):
        return bass.AP(dram_ap.tensor, 0, [[width, 128], [128 * width, ntiles], [1, width]])

    def col_ap(dram_ap, n_per, ntiles):
        return bass.AP(dram_ap.tensor, 0, [[n_per, 128], [128 * n_per, ntiles], [1, n_per]])

    bk_w = [None] * 8
    bk_r = [None] * 8

    def vstep(v, ins):
        sV.inc(ins)
        v.wait_ge(sV.h, sV.n)

    def bwr(b, sem):
        bk_w[b] = (sem, sem.n)

    def brd(b, sem):
        bk_r[b] = (sem, sem.n)

    def wait_bank_free(eng, b):
        if bk_r[b] is not None:
            s, tok = bk_r[b]
            s.wait(eng, tok)

    def wait_bank_data(eng, b):
        if bk_w[b] is not None:
            s, tok = bk_w[b]
            s.wait(eng, tok)

    @blk.sync
    def _(sy):
        for dst, src in [(ident_b[:], din["c_ident_b"]), (ident_f[:], din["c_ident_f"]),
                         (ones_b[:], din["c_ones_b"]), (ones_f[:], din["c_ones_f"]),
                         (sel_t[:], din["c_sel"]), (mask_t[:], din["c_mask"]),
                         (sins0[:], din["c_sins"][0:128, :]),
                         (sins1[:], din["c_sins"][128:256, :])]:
            sy.dma_start(out=dst, in_=src).then_inc(sD.h, 16)
            sD.n += 16
        for dc, src, npc, nt in [(SM_LNG, din["ln_g"], 1, 8), (SM_LNB, din["ln_b"], 1, 8),
                                 (SM_MUR, din["mu_r"], 1, 8), (SM_MUK, din["mu_k"], 1, 8),
                                 (SM_CB, din["conv_b"], 1, 16), (SM_BD, din["b_delta"], 1, 16),
                                 (SM_DP, din["dp"], 1, 16), (SM_OB, din["out_b"], 1, 8),
                                 (SM_RMS, din["rms_t"], NE, 8), (SM_RMSS, din["rms_s_t"], 2, 8)]:
            sy.dma_start(out=smalls[:, dc:dc + npc * nt], in_=col_ap(src, npc, nt), allow_slow_non_contiguous=True).then_inc(sD.h, 16)
            sD.n += 16
        sy.dma_start(out=smalls[:, SM_QB:SM_QB + NH], in_=din["qb_dup"]).then_inc(sD.h, 16)
        sD.n += 16
        sy.dma_start(out=convw_t[:], in_=col_ap(din["conv_w"], 4, 16), allow_slow_non_contiguous=True).then_inc(sD.h, 16)
        sD.n += 16
        sy.dma_start(out=aneg_t[:], in_=col_ap(din["log_a"], DS, 16), allow_slow_non_contiguous=True).then_inc(sD.h, 16)
        sD.n += 16
        sy.dma_start(out=cpool[:], in_=din["c_pool"]).then_inc(sD.h, 16)
        sD.n += 16
        sy.dma_start(out=f32(A_XRES, 8 * T), in_=fm_ap(din["x_t"], 8, T)).then_inc(sD.h, 16)
        sD.n += 16

    @blk.gpsimd
    def _(g):
        for dst, src in [(scr_wdelta, din["w_delta"]), (scr_winz, din["w_in"][:, DI:2 * DI]),
                         (scr_wout, din["w_out"]), (scr_wr, din["w_r"]),
                         (scr_wk, din["w_k"]), (scr_wv, din["w_v"])]:
            g.dma_start(out=dst, in_=src).then_inc(sG.h, 16)
            sG.n += 16
        g.dma_start(out=wb_t[:], in_=col_ap(din["w_b"], DS, 16)).then_inc(sG.h, 16)
        sG.n += 16
        g.dma_start(out=wc_t[:], in_=col_ap(din["w_c"], DS, 16)).then_inc(sG.h, 16)
        sG.n += 16

    @blk.vector
    def _(v):
        sD.wait(v)
        ins = v.memset(bf(A_TMP1, 512), 0.0)
        sV.inc(ins)

    @blk.sync
    def _(sy):
        sV.wait(sy)
        for rb in range(16):
            sy.dma_start(out=scr_pos[rb * 128:(rb + 1) * 128, T:POSW],
                         in_=bf(A_TMP1, 512)).then_inc(sD.h, 16)
            sD.n += 16

    @blk.scalar
    def _(s):
        sD.wait(s)
        ins = s.activation(out=aneg_t[:], in_=aneg_t[:], func=AF.Exp)
        sS.inc(ins)

    @blk.vector
    def _(v):
        sS.wait(v)
        v.tensor_scalar_mul(aneg_t[:], aneg_t[:], -1.0)
        v.memset(hcar_t[:], 0.0)
        v.tensor_scalar(smalls[:, SM_OMR:SM_OMR + 8], smalls[:, SM_MUR:SM_MUR + 8],
                        -1.0, 1.0, ALU.mult, ALU.add)
        ins = v.tensor_scalar(smalls[:, SM_OMK:SM_OMK + 8], smalls[:, SM_MUK:SM_MUK + 8],
                              -1.0, 1.0, ALU.mult, ALU.add)
        sV.inc(ins)

    def ln_stats(xsl, n_tiles, eps, f32_in, sq_off, do_mu=True):
        onesc = ones_f[:, 0:1] if f32_in else ones_b[:, 0:1]
        sq = [f32(sq_off, 512) if f32_in else bf(sq_off, 512),
              f32(sq_off + 2048, 512) if f32_in else bf(sq_off + 2048, 512)]
        rd_tok = [sT.n, sT.n]
        for ch in range(4):
            b1, b2 = ch, 4 + ch
            for ft in range(n_tiles):
                par = ft % 2

                @blk.scalar
                def _(s, ch=ch, ft=ft, par=par, tok=rd_tok[par]):
                    s.wait_ge(sT.h, tok)
                    if ch == 0 and ft == 0:
                        sD.wait(s)
                        sV.wait(s)
                    ins = s.activation(out=sq[par], in_=xsl(ft, ch * 512, 512), func=AF.Square)
                    sS.inc(ins)

                @blk.tensor
                def _(t, ch=ch, ft=ft, par=par, b1=b1, b2=b2):
                    sS.wait(t)
                    if ft == 0:
                        wait_bank_free(t, b1)
                        wait_bank_free(t, b2)
                    if do_mu:
                        t.matmul(bank(b1, 512, rows=1), onesc, xsl(ft, ch * 512, 512),
                                 start=(ft == 0), stop=(ft == n_tiles - 1))
                    ins = t.matmul(bank(b2, 512, rows=1), onesc, sq[par],
                                   start=(ft == 0), stop=(ft == n_tiles - 1))
                    sT.inc(ins)
                    if ft == n_tiles - 1:
                        bwr(b1, sT)
                        bwr(b2, sT)
                rd_tok[par] = sT.n

        @blk.vector
        def _(v):
            sT.wait(v)
            n = float(n_tiles * 128)
            for ch in range(4):
                b1, b2 = ch, 4 + ch
                if do_mu:
                    v.tensor_copy(rowbuf[:, RB_MU + ch * 512:RB_MU + ch * 512 + 512],
                                  bank(b1, 512, rows=1))
                ins = v.tensor_copy(rowbuf[:, RB_RS + ch * 512:RB_RS + ch * 512 + 512],
                                    bank(b2, 512, rows=1))
            sV.inc(ins)
            for b in range(8):
                brd(b, sV)
            r = rowbuf[:, RB_RS:RB_RS + T]
            mu = rowbuf[:, RB_MU:RB_MU + T]
            ins = v.tensor_scalar_mul(r, r, 1.0 / n)
            if do_mu:
                v.tensor_scalar_mul(mu, mu, 1.0 / n)
                for ch in range(4):
                    sc = rowbuf[:, RB_SC:RB_SC + 512]
                    mch = rowbuf[:, RB_MU + ch * 512:RB_MU + ch * 512 + 512]
                    rch = rowbuf[:, RB_RS + ch * 512:RB_RS + ch * 512 + 512]
                    v.tensor_mul(sc, mch, mch)
                    ins = v.tensor_sub(rch, rch, sc)
            sV.inc(ins)

        @blk.scalar
        def _(s):
            sV.wait(s)
            r = rowbuf[:, RB_RS:RB_RS + T]
            s.activation(out=r, in_=r, func=AF.Ln, bias=float(eps))
            ins = s.activation(out=r, in_=r, func=AF.Exp, scale=-0.5)
            sS.inc(ins)

        @blk.vector
        def _(v):
            sS.wait(v)
            if do_mu:
                ins = v.tensor_mul(rowbuf[:, RB_MU:RB_MU + T], rowbuf[:, RB_MU:RB_MU + T],
                                   rowbuf[:, RB_RS:RB_RS + T])
            else:
                ins = v.tensor_copy(rowbuf[:, RB_SC + 500:RB_SC + 501],
                                    rowbuf[:, RB_RS:RB_RS + 1])
            sV.inc(ins)

    def ln_apply(xsl, dsl, n_tiles, tmp_off, gcol=None, bcol=None, sub_mu=True):
        for ch in range(4):
            b1 = 4 + (ch % 2) * 2
            b2 = b1 + 1

            @blk.tensor
            def _(t, ch=ch, b1=b1, b2=b2):
                sV.wait(t)
                wait_bank_free(t, b1)
                wait_bank_free(t, b2)
                ins = t.matmul(bank(b1, 512), ones_f[0:1, :],
                               rowbuf[:, RB_RS + ch * 512:RB_RS + ch * 512 + 512],
                               start=True, stop=True)
                if sub_mu:
                    ins = t.matmul(bank(b2, 512), ones_f[0:1, :],
                                   rowbuf[:, RB_MU + ch * 512:RB_MU + ch * 512 + 512],
                                   start=True, stop=True)
                sT.inc(ins)
                bwr(b1, sT)
                if sub_mu:
                    bwr(b2, sT)

            @blk.vector
            def _(v, ch=ch, b1=b1, b2=b2):
                sT.wait(v)
                tmp = f32(tmp_off, 512)
                for ft in range(n_tiles):
                    xi = xsl(ft, ch * 512, 512)
                    do = dsl(ft, ch * 512, 512)
                    if gcol is None:
                        if sub_mu:
                            v.tensor_mul(tmp, xi, bank(b1, 512))
                            ins = v.tensor_sub(do, tmp, bank(b2, 512))
                        else:
                            ins = v.tensor_mul(do, xi, bank(b1, 512))
                    else:
                        v.tensor_mul(tmp, xi, bank(b1, 512))
                        v.tensor_sub(tmp, tmp, bank(b2, 512))
                        ins = v.tensor_scalar(do, tmp, gcol[ft], bcol[ft], ALU.mult, ALU.add)
                sV.inc(ins)
                brd(b1, sV)
                brd(b2, sV)

    XN_W = 2052

    def ph1():
        xres = f32(A_XRES, 8 * T)
        xn = bf(A_XN, 8 * XN_W)
        barrier()
        ln_stats(lambda ft, c, w: xres[:, ft * T + c:ft * T + c + w], 8, LN_EPS, True, A_SQ)

        @blk.vector
        def _(v):
            for ft in range(8):
                ins = v.memset(xn[:, ft * XN_W:ft * XN_W + 4], 0.0)
            sV.inc(ins)

        ln_apply(lambda ft, c, w: xres[:, ft * T + c:ft * T + c + w],
                 lambda ft, c, w: xn[:, ft * XN_W + 4 + c:ft * XN_W + 4 + c + w],
                 8, A_TMP1)

        @blk.vector
        def _(v):
            for ft in range(8):
                ins = v.tensor_add(xres[:, ft * T:(ft + 1) * T], xres[:, ft * T:(ft + 1) * T],
                                   xn[:, ft * XN_W + 4:ft * XN_W + 4 + T])
            sV.inc(ins)

        @blk.sync
        def _(sy):
            sV.wait(sy)
            sy.dma_start(out=fm_ap(scr_xres, 8, T), in_=xres).then_inc(sD.h, 16)
            sD.n += 16
            for ft in range(8):
                sy.dma_start(out=scr_xn[ft * 128:(ft + 1) * 128, :],
                             in_=xn[:, ft * XN_W + 4:ft * XN_W + 4 + T]).then_inc(sD.h, 16)
                sD.n += 16

        if stage in ("ln0", "ffn"):
            return
        xnm = bf(A_XNM, 8 * T)
        ln_stats(lambda ft, c, w: xn[:, ft * XN_W + 4 + c:ft * XN_W + 4 + c + w],
                 8, LN_EPS, False, A_SQ)
        gc = [smalls[:, SM_LNG + ft:SM_LNG + ft + 1] for ft in range(8)]
        bc = [smalls[:, SM_LNB + ft:SM_LNB + ft + 1] for ft in range(8)]
        ln_apply(lambda ft, c, w: xn[:, ft * XN_W + 4 + c:ft * XN_W + 4 + c + w],
                 lambda ft, c, w: xnm[:, ft * T + c:ft * T + c + w],
                 8, A_TMP1, gcol=gc, bcol=bc)

    def ph2():
        barrier()
        xk = bf(A_XK, 8 * T)
        kact = bf(A_KACT, 8 * T)
        xstr = [bf(A_XNS, XN_W), bf(A_XNS + 4104, XN_W)]

        @blk.vector
        def _(v):
            v.memset(xstr[0][:, 0:4], 0.0)
            ins = v.memset(xstr[1][:, 0:4], 0.0)
            sV.inc(ins)

        def mix(dst, mcol, omcol):
            toks = {}
            for ft in range(8):
                par = ft % 2

                @blk.sync
                def _(sy, ft=ft, par=par):
                    if ft >= 2:
                        sV.wait(sy, toks[ft - 2])
                    sy.dma_start(out=xstr[par][:, 4:4 + T],
                                 in_=scr_xn[ft * 128:(ft + 1) * 128, :]).then_inc(sD.h, 16)
                    sD.n += 16

                @blk.vector
                def _(v, ft=ft, par=par):
                    sD.wait(v)
                    v.tensor_scalar(dst[:, ft * T:(ft + 1) * T], xstr[par][:, 4:4 + T],
                                    mcol[ft], None, ALU.mult)
                    ins = v.scalar_tensor_tensor(dst[:, ft * T:(ft + 1) * T],
                                                 xstr[par][:, 3:3 + T], omcol[ft],
                                                 dst[:, ft * T:(ft + 1) * T],
                                                 ALU.mult, ALU.add)
                    sV.inc(ins)
                    toks[ft] = sV.n

        mcolr = [smalls[:, SM_MUR + f:SM_MUR + f + 1] for f in range(8)]
        omcolr = [smalls[:, SM_OMR + f:SM_OMR + f + 1] for f in range(8)]
        mcolk = [smalls[:, SM_MUK + f:SM_MUK + f + 1] for f in range(8)]
        omcolk = [smalls[:, SM_OMK + f:SM_OMK + f + 1] for f in range(8)]
        mix(xk, mcolk, omcolk)

        @blk.sync
        def _(sy):
            sG.wait(sy)
            for ct in range(8):
                sy.dma_start(out=bf(A_WT + ct * 2048, D),
                             in_=scr_wk[:, ct * 128:(ct + 1) * 128], transpose=True).then_inc(sD.h, 16)
                sD.n += 16
        wtok = sD.n

        for dt in range(8):
            for ch in range(4):
                b = ch % 2

                @blk.tensor
                def _(t, dt=dt, ch=ch, b=b):
                    if dt == 0 and ch == 0:
                        sD.wait(t, wtok)
                        sV.wait(t)
                    wait_bank_free(t, b)
                    for kt in range(8):
                        ins = t.matmul(bank(b, 512), bf(A_WT + kt * 2048 + dt * 256, 128),
                                       xk[:, kt * T + ch * 512:kt * T + ch * 512 + 512],
                                       start=(kt == 0), stop=(kt == 7))
                    sT.inc(ins)
                    bwr(b, sT)

                @blk.scalar
                def _(s, dt=dt, ch=ch, b=b):
                    sT.wait(s)
                    s.activation(out=bf(A_TMP2, 512), in_=bank(b, 512), func=AF.Relu)
                    ins = s.activation(out=kact[:, dt * T + ch * 512:dt * T + ch * 512 + 512],
                                       in_=bf(A_TMP2, 512), func=AF.Square)
                    sS.inc(ins)
                    brd(b, sS)

        barrier()
        xr = bf(A_XK, 8 * T)
        mix(xr, mcolr, omcolr)

        @blk.sync
        def _(sy):
            for ct in range(8):
                sy.dma_start(out=bf(A_WT + ct * 2048, D),
                             in_=scr_wr[:, ct * 128:(ct + 1) * 128], transpose=True).then_inc(sD.h, 16)
                sD.n += 16
            for ct in range(8):
                sy.dma_start(out=bf(A_WT2 + ct * 2048, D),
                             in_=scr_wv[:, ct * 128:(ct + 1) * 128], transpose=True).then_inc(sD.h, 16)
                sD.n += 16
        wtok2 = sD.n

        for dt in range(8):
            for ch in range(4):
                br = 2 + (ch % 2)
                bv = ch % 2

                @blk.tensor
                def _(t, dt=dt, ch=ch, br=br, bv=bv):
                    if dt == 0 and ch == 0:
                        sD.wait(t, wtok2)
                        sV.wait(t)
                    wait_bank_free(t, br)
                    for kt in range(8):
                        ins = t.matmul(bank(br, 512), bf(A_WT + kt * 2048 + dt * 256, 128),
                                       xr[:, kt * T + ch * 512:kt * T + ch * 512 + 512],
                                       start=(kt == 0), stop=(kt == 7))
                    sT.inc(ins)
                    bwr(br, sT)
                    wait_bank_free(t, bv)
                    for kt in range(8):
                        ins = t.matmul(bank(bv, 512), bf(A_WT2 + kt * 2048 + dt * 256, 128),
                                       kact[:, kt * T + ch * 512:kt * T + ch * 512 + 512],
                                       start=(kt == 0), stop=(kt == 7))
                    sT.inc(ins)
                    bwr(bv, sT)

                @blk.scalar
                def _(s, dt=dt, ch=ch, br=br):
                    sT.wait(s)
                    sV.wait(s)
                    ins = s.activation(out=bf(A_TMP2, 512), in_=bank(br, 512), func=AF.Sigmoid)
                    sS.inc(ins)
                    brd(br, sS)

                @blk.vector
                def _(v, dt=dt, ch=ch, bv=bv):
                    sS.wait(v)
                    sG.wait(v)
                    ins = v.tensor_mul(f32(A_TMP2 + 2048, 512), bank(bv, 512), bf(A_TMP2, 512))
                    sV.inc(ins)
                    brd(bv, sV)

                @blk.gpsimd
                def _(g, dt=dt, ch=ch):
                    sV.wait(g)
                    g.dma_start(out=scr_xres[dt * 128:(dt + 1) * 128, ch * 512:(ch + 1) * 512],
                                in_=f32(A_TMP2 + 2048, 512), accum_op=ALU.add).then_inc(sG.h, 16)
                    sG.n += 16

    def ph3():
        barrier()
        xnm = bf(A_XNM, 8 * T)
        xb = bf(A_XB, 16 * T)
        winx = bf(A_WINX, 8 * T)
        xbr = f32(A_XBR, 2056)
        ct1 = f32(A_CT, T)

        @blk.gpsimd
        def _(g):
            for kt in range(8):
                g.dma_start(out=bf(A_WINX + kt * 4096, DI),
                            in_=din["w_in"][kt * 128:(kt + 1) * 128, 0:DI]).then_inc(sG.h, 16)
                sG.n += 16
        wtok = sG.n

        @blk.vector
        def _(v):
            ins = v.memset(xbr[:, 0:4], 0.0)
            sV.inc(ins)

        for mt in range(16):
            for ch in range(4):
                b = ch % 2

                @blk.tensor
                def _(t, mt=mt, ch=ch, b=b):
                    if mt == 0 and ch == 0:
                        sG.wait(t, wtok)
                    wait_bank_free(t, b)
                    for kt in range(8):
                        ins = t.matmul(bank(b, 512), bf(A_WINX + kt * 4096 + mt * 256, 128),
                                       xnm[:, kt * T + ch * 512:kt * T + ch * 512 + 512],
                                       start=(kt == 0), stop=(kt == 7))
                    sT.inc(ins)
                    bwr(b, sT)

                @blk.vector
                def _(v, mt=mt, ch=ch, b=b):
                    sT.wait(v)
                    sS.wait(v)
                    ins = v.tensor_copy(xbr[:, 4 + ch * 512:4 + ch * 512 + 512], bank(b, 512))
                    sV.inc(ins)
                    brd(b, sV)

            @blk.vector
            def _(v, mt=mt):
                cw = [convw_t[:, mt * 4 + k:mt * 4 + k + 1] for k in range(4)]
                v.tensor_scalar(ct1, xbr[:, 1:1 + T], cw[0], None, ALU.mult)
                for k in (1, 2):
                    v.scalar_tensor_tensor(ct1, xbr[:, 1 + k:1 + k + T], cw[k], ct1,
                                           ALU.mult, ALU.add)
                ins = v.scalar_tensor_tensor(ct1, xbr[:, 4:4 + T], cw[3], ct1,
                                             ALU.mult, ALU.add)
                sV.inc(ins)

            @blk.scalar
            def _(s, mt=mt):
                sV.wait(s)
                ins = s.activation(out=xb[:, mt * T:(mt + 1) * T], in_=ct1, func=AF.Silu,
                                   bias=smalls[:, SM_CB + mt:SM_CB + mt + 1])
                sS.inc(ins)

        for which, wt, coff in ((0, wb_t, 0), (1, wc_t, T)):
            for ch in range(4):
                b = 4 + (which * 2 + ch) % 2

                @blk.tensor
                def _(t, wt=wt, ch=ch, b=b):
                    sS.wait(t)
                    wait_bank_free(t, b)
                    for kt in range(16):
                        ins = t.matmul(bank(b, 512, rows=16), wt[:, kt * 16:(kt + 1) * 16],
                                       xb[:, kt * T + ch * 512:kt * T + ch * 512 + 512],
                                       start=(kt == 0), stop=(kt == 15))
                    sT.inc(ins)
                    bwr(b, sT)

                @blk.vector
                def _(v, ch=ch, b=b, coff=coff):
                    sT.wait(v)
                    ins = v.tensor_copy(bmcm[0:16, coff + ch * 512:coff + ch * 512 + 512],
                                        bank(b, 512, rows=16))
                    sV.inc(ins)
                    brd(b, sV)

    def ph4():
        barrier()
        xnm = bf(A_XNM, 8 * T)
        xb = bf(A_XB, 16 * T)
        TC = 256
        delta = f32(A_DELTA, 16 * TC)
        dxb = bf(A_DXB, 16 * TC)
        yac = f32(A_Y, 16 * TC)
        strm = [bf(A_STRM, 2048), bf(A_STRM + 4096, 2048)]
        bcB = f32(A_SC, TC)
        bcC = f32(A_SC + 1024, TC)
        abar = [f32(A_SC + 2048, TC), f32(A_SC + 3072, TC)]
        bx = f32(A_SC + 4096, TC)
        hbuf = [f32(A_SC + 5120, TC), f32(A_SC + 6144, TC)]
        zs = [bf(A_SC + 6144, TC), bf(A_SC + 6656, TC)]
        mo = [f32(A_SC + 7168, TC), f32(A_SC // 1 + 0, 0)]
        mo = f32(A_SC + 7168, TC)

        exp_tmp = f32(A_DELTA + 16 * TC * 4 - 4096, 0)

        for tc in range(8):
            t0 = tc * TC
            for kt in range(16):
                par = kt % 2

                @blk.sync
                def _(sy, tc=tc, kt=kt, par=par, tok=strm_last[par]):
                    if tc == 0 and kt == 0:
                        sG.wait(sy)
                    sT.wait(sy, tok)
                    sy.dma_start(out=strm[par],
                                 in_=scr_wdelta[kt * 128:(kt + 1) * 128, :]).then_inc(sD.h, 16)
                    sD.n += 16

                @blk.tensor
                def _(t, tc=tc, kt=kt, par=par):
                    sD.wait(t)
                    if kt == 0:
                        for b in range(8):
                            wait_bank_free(t, b)
                    for mt in range(16):
                        ins = t.matmul(bank(mt // 2, TC, col0=(mt % 2) * TC),
                                       strm[par][:, mt * 128:(mt + 1) * 128],
                                       xb[:, kt * T + t0:kt * T + t0 + TC],
                                       start=(kt == 0), stop=(kt == 15))
                    if kt == 0:
                        for mt in range(0, 16, 2):
                            ins = t.matmul(bank(mt // 2, TC, col0=0),
                                           strm[par][:, mt * 128:(mt + 1) * 128],
                                           xb[:, kt * T + t0:kt * T + t0 + TC],
                                           start=False, stop=False)
                    sT.inc(ins)
                strm_last[par] = sT.n
            for b in range(8):
                bwr(b, sT)

            @blk.scalar
            def _(s, tc=tc):
                sT.wait(s)
                sG.wait(s)
                for mt in range(16):
                    bd = smalls[:, SM_BD + mt:SM_BD + mt + 1]
                    s.activation(out=f32(A_SC + 7168, TC), in_=bank(mt // 2, TC, col0=(mt % 2) * TC),
                                 func=AF.Exp, bias=bd)
                    ins = s.activation(out=delta[:, mt * TC:(mt + 1) * TC],
                                       in_=f32(A_SC + 7168, TC), func=AF.Ln, bias=1.0)
                sS.inc(ins)
                for b in range(8):
                    brd(b, sS)

            @blk.vector
            def _(v, tc=tc):
                sS.wait(v)
                for mt in range(16):
                    ins = v.tensor_mul(dxb[:, mt * TC:(mt + 1) * TC],
                                       delta[:, mt * TC:(mt + 1) * TC],
                                       xb[:, mt * T + t0:mt * T + t0 + TC])
                sV.inc(ins)
            if _PH4 < 2:
                continue

            ab_tok = [sV.n, sV.n]
            scan_tok = {}
            for s_i in range(DS):
                @blk.tensor
                def _(t, tc=tc, s_i=s_i):
                    sV.wait(t)
                    wait_bank_free(t, 0)
                    t.matmul(bank(0, TC), sel_t[:, s_i * 128:(s_i + 1) * 128],
                             bmcm[0:16, t0:t0 + TC], start=True, stop=True)
                    ins = t.matmul(bank(0, TC, col0=TC), sel_t[:, s_i * 128:(s_i + 1) * 128],
                                   bmcm[0:16, T + t0:T + t0 + TC], start=True, stop=True)
                    sT.inc(ins)
                    bwr(0, sT)

                @blk.vector
                def _(v, tc=tc, s_i=s_i):
                    sT.wait(v)
                    v.tensor_copy(bcB, bank(0, TC))
                    ins = v.tensor_copy(bcC, bank(0, TC, col0=TC))
                    sV.inc(ins)
                    brd(0, sV)

                for dc in range(16):
                    par = dc % 2
                    k = s_i * 16 + dc

                    @blk.scalar
                    def _(s, tc=tc, s_i=s_i, dc=dc, par=par, k=k, tok=ab_tok[par]):
                        s.wait_ge(sV.h, tok)
                        if k >= 2 and tc < 7:
                            s2, d2 = divmod(k - 2, 16)
                            s.activation(out=hcar_t[:, d2 * 16 + s2:d2 * 16 + s2 + 1],
                                         in_=hbuf[par][:, TC - 1:TC], func=AF.Copy)
                        ins = s.activation(out=abar[par], in_=delta[:, dc * TC:(dc + 1) * TC],
                                           func=AF.Exp,
                                           scale=aneg_t[:, dc * 16 + s_i:dc * 16 + s_i + 1])
                        sS.inc(ins)

                    @blk.vector
                    def _(v, tc=tc, s_i=s_i, dc=dc, par=par):
                        sS.wait(v)
                        v.tensor_mul(bx, dxb[:, dc * TC:(dc + 1) * TC], bcB)
                        v.tensor_tensor_scan(hbuf[par], abar[par], bx,
                                             hcar_t[:, dc * 16 + s_i:dc * 16 + s_i + 1],
                                             ALU.mult, ALU.add)
                        ysl = yac[:, dc * TC:(dc + 1) * TC]
                        if s_i == 0:
                            ins = v.tensor_mul(ysl, hbuf[par], bcC)
                        else:
                            v.tensor_mul(bx, hbuf[par], bcC)
                            ins = v.tensor_add(ysl, ysl, bx)
                        sV.inc(ins)
                    ab_tok[par] = sV.n
                    scan_tok[k] = sV.n

            if tc < 7:
                @blk.scalar
                def _(s, tc=tc, t254=scan_tok[254], t255=scan_tok[255]):
                    s.wait_ge(sV.h, t254)
                    s.activation(out=hcar_t[:, 14 * 16 + 15:14 * 16 + 15 + 1],
                                 in_=hbuf[0][:, TC - 1:TC], func=AF.Copy)
                    s.wait_ge(sV.h, t255)
                    ins = s.activation(out=hcar_t[:, 15 * 16 + 15:15 * 16 + 15 + 1],
                                       in_=hbuf[1][:, TC - 1:TC], func=AF.Copy)
                    sS.inc(ins)

            if _PH4 < 3:
                continue
            @blk.vector
            def _(v, tc=tc):
                for mt in range(16):
                    ins = v.scalar_tensor_tensor(yac[:, mt * TC:(mt + 1) * TC],
                                                 xb[:, mt * T + t0:mt * T + t0 + TC],
                                                 smalls[:, SM_DP + mt:SM_DP + mt + 1],
                                                 yac[:, mt * TC:(mt + 1) * TC],
                                                 ALU.mult, ALU.add)
                sV.inc(ins)

            for kt in range(8):
                par = kt % 2

                @blk.sync
                def _(sy, tc=tc, kt=kt, par=par, tok=strm_last[par]):
                    sT.wait(sy, tok)
                    sy.dma_start(out=strm[par],
                                 in_=scr_winz[kt * 128:(kt + 1) * 128, :]).then_inc(sD.h, 16)
                    sD.n += 16

                @blk.tensor
                def _(t, tc=tc, kt=kt, par=par):
                    sD.wait(t)
                    if kt == 0:
                        sV.wait(t)
                        for b in range(8):
                            wait_bank_free(t, b)
                    for mt in range(16):
                        ins = t.matmul(bank(mt // 2, TC, col0=(mt % 2) * TC),
                                       strm[par][:, mt * 128:(mt + 1) * 128],
                                       xnm[:, kt * T + t0:kt * T + t0 + TC],
                                       start=(kt == 0), stop=(kt == 7))
                    if kt == 0:
                        for mt in range(0, 16, 2):
                            ins = t.matmul(bank(mt // 2, TC, col0=0),
                                           strm[par][:, mt * 128:(mt + 1) * 128],
                                           xnm[:, kt * T + t0:kt * T + t0 + TC],
                                           start=False, stop=False)
                    sT.inc(ins)
                strm_last[par] = sT.n
            for b in range(8):
                bwr(b, sT)

            yg = dxb
            for mt in range(16):
                par = mt % 2

                @blk.scalar
                def _(s, tc=tc, mt=mt, par=par, tok=None):
                    sT.wait(s)
                    if mt >= 2:
                        s.wait_ge(sV.h, yg_tok.get((tc, mt - 2), 0))
                    ins = s.activation(out=zs[par], in_=bank(mt // 2, TC, col0=(mt % 2) * TC),
                                       func=AF.Silu)
                    sS.inc(ins)
                    if mt == 15:
                        for b in range(8):
                            brd(b, sS)

                @blk.vector
                def _(v, tc=tc, mt=mt, par=par):
                    sS.wait(v)
                    ins = v.tensor_mul(yg[:, mt * TC:(mt + 1) * TC],
                                       yac[:, mt * TC:(mt + 1) * TC], zs[par])
                    sV.inc(ins)
                    yg_tok[(tc, mt)] = sV.n

            if stage == "dbg2" and tc == 0:
                @blk.gpsimd
                def _(g):
                    sV.wait(g)
                    sS.wait(g)
                    g.dma_start(out=dbg[:, 0:4096], in_=delta).then_inc(sG.h, 16)
                    g.dma_start(out=dbg[:, 4096:8192], in_=yac).then_inc(sG.h, 16)
                    g.dma_start(out=dbg[:, 8192:12288], in_=yg).then_inc(sG.h, 16)
                    g.dma_start(out=dbg[:, 12288:12544], in_=hcar_t[:]).then_inc(sG.h, 16)
                    sG.n += 64
            if _PH4 < 4:
                continue
            for kt in range(16):
                par = kt % 2

                @blk.sync
                def _(sy, tc=tc, kt=kt, par=par, tok=strm_last[par]):
                    sT.wait(sy, tok)
                    sy.dma_start(out=strm[par][:, 0:1024],
                                 in_=scr_wout[kt * 128:(kt + 1) * 128, :]).then_inc(sD.h, 16)
                    sD.n += 16

                @blk.tensor
                def _(t, tc=tc, kt=kt, par=par):
                    sD.wait(t)
                    if kt == 0:
                        sV.wait(t)
                        for b in range(4):
                            wait_bank_free(t, b)
                    for dt in range(8):
                        ins = t.matmul(bank(dt // 2, TC, col0=(dt % 2) * TC),
                                       strm[par][:, dt * 128:(dt + 1) * 128],
                                       yg[:, kt * TC:(kt + 1) * TC],
                                       start=(kt == 0), stop=(kt == 15))
                    if kt == 0:
                        for dt in range(0, 8, 2):
                            ins = t.matmul(bank(dt // 2, TC, col0=0),
                                           strm[par][:, dt * 128:(dt + 1) * 128],
                                           yg[:, kt * TC:(kt + 1) * TC],
                                           start=False, stop=False)
                    sT.inc(ins)
                strm_last[par] = sT.n
            for b in range(4):
                bwr(b, sT)

            for dt in range(8):
                @blk.vector
                def _(v, tc=tc, dt=dt):
                    sT.wait(v)
                    sG.wait(v)
                    ins = v.tensor_copy(mo, bank(dt // 2, TC, col0=(dt % 2) * TC))
                    sV.inc(ins)
                    if dt == 7:
                        for b in range(4):
                            brd(b, sV)

                @blk.gpsimd
                def _(g, tc=tc, dt=dt):
                    sV.wait(g)
                    g.dma_start(out=scr_mo[dt * 128:(dt + 1) * 128, t0:t0 + TC],
                                in_=mo).then_inc(sG.h, 16)
                    sG.n += 16
                    mo_tok[(tc, dt)] = sG.n

    strm_tok = {}
    strm_last = [0, 0]
    yg_tok = {}
    mo_tok = {}

    def reload_xres(xres, tmp_off, add_mo, respill):
        @blk.sync
        def _(sy):
            sG.wait(sy)
            sy.dma_start(out=xres, in_=fm_ap(scr_xres, 8, T)).then_inc(sD.h, 16)
            sD.n += 16

        if add_mo:
            for ft in range(8):
                @blk.sync
                def _(sy, ft=ft):
                    sV.wait(sy)
                    sy.dma_start(out=f32(tmp_off, T),
                                 in_=scr_mo[ft * 128:(ft + 1) * 128, :]).then_inc(sD.h, 16)
                    sD.n += 16

                @blk.vector
                def _(v, ft=ft):
                    sD.wait(v)
                    ins = v.tensor_add(xres[:, ft * T:(ft + 1) * T],
                                       xres[:, ft * T:(ft + 1) * T], f32(tmp_off, T))
                    sV.inc(ins)

        if respill:
            @blk.sync
            def _(sy):
                sV.wait(sy)
                sy.wait_ge(sD.h, sD.n)
                sy.dma_start(out=fm_ap(scr_xres, 8, T), in_=xres).then_inc(sD.h, 16)
                sD.n += 16

    def ph5():
        barrier()
        xres = f32(A_XRES1, 8 * T)
        xn1 = bf(A_XNM, 8 * T)
        reload_xres(xres, A_SQ1, stage not in ("ln0", "ffn", "pre"), True)
        if stage == "dbg5":
            barrier()

            @blk.gpsimd
            def _(g):
                g.dma_start(out=dbg[:, 20480:20480 + 16384], in_=xres).then_inc(sG.h, 16)
                sG.n += 16
            barrier()
            return

        ln_stats(lambda ft, c, w: xres[:, ft * T + c:ft * T + c + w], 8, LN_EPS, True, A_SQ1)
        ln_apply(lambda ft, c, w: xres[:, ft * T + c:ft * T + c + w],
                 lambda ft, c, w: xn1[:, ft * T + c:ft * T + c + w],
                 8, A_TMP5)

    def ph6():
        barrier()
        xn1 = bf(A_XNM, 8 * T)
        ohs = bf(A_OH, 16 * T)
        Qp = bf(A_QP, T)
        Kp = bf(A_KP, T)
        Vp = bf(A_VP, 16 * 128)
        emb = bf(A_EMB, POSW)
        posb = bf(A_REGA, POSW)
        PTb = bf(A_REGA, T)
        diag = bf(A_REGA + 4096, 128)
        Pb = bf(A_P, T)
        QQ = bf(A_P, T)
        Sf = [f32(A_SF, 512), f32(A_SF + 2048, 512)]
        relb = [bf(A_REL, 512), bf(A_REL + 1024, 512)]
        rsums = f32(A_SP6, 8)
        negc = [f32(A_SP6 + 64, 1), f32(A_SP6 + 128, 1)]
        ncb = bf(A_SP6 + 192, 2)
        qkvw = bf(A_WS6, 8 * 128)
        posw = bf(A_WS6 + 2048, 2 * 128)

        for hp in range(8):
            for a in range(3):
                @blk.gpsimd
                def _(g, hp=hp, a=a):
                    if hp > 0 or a > 0:
                        sT.wait(g, qkv_tok.get((hp, a), 0))
                    src = din["qkv_f"]
                    c0 = a * 1024 + hp * 128
                    g.dma_start(out=qkvw,
                                in_=bass.AP(src.tensor, c0,
                                            [[3072, 128], [128 * 3072, 8], [1, 128]])).then_inc(sG.h, 16)
                    sG.n += 16

                if a < 2:
                    dst = Qp if a == 0 else Kp
                    sc = WSC * HSC
                    for ch in range(4):
                        b = ch % 2

                        @blk.tensor
                        def _(t, hp=hp, a=a, ch=ch, b=b):
                            sG.wait(t)
                            if hp == 0 and a == 0 and ch == 0:
                                sV.wait(t)
                            wait_bank_free(t, b)
                            for kt in range(8):
                                ins = t.matmul(bank(b, 512), qkvw[:, kt * 128:(kt + 1) * 128],
                                               xn1[:, kt * T + ch * 512:kt * T + ch * 512 + 512],
                                               start=(kt == 0), stop=(kt == 7))
                            sT.inc(ins)
                            bwr(b, sT)

                        @blk.scalar
                        def _(s, dst=dst, sc=sc, ch=ch, b=b):
                            sT.wait(s)
                            ins = s.activation(out=dst[:, ch * 512:ch * 512 + 512],
                                               in_=bank(b, 512), func=AF.Copy, scale=sc)
                            sS.inc(ins)
                            brd(b, sS)
                    qkv_tok[(hp, a + 1)] = sT.n
                else:
                    for tg in range(4):
                        b = 2 + tg % 2

                        @blk.tensor
                        def _(t, hp=hp, tg=tg, b=b):
                            sG.wait(t)
                            wait_bank_free(t, b)
                            for kt in range(8):
                                for tt in range(tg * 4, tg * 4 + 4):
                                    ins = t.matmul(bank(b, 128, col0=(tt % 4) * 128),
                                                   xn1[:, kt * T + tt * 128:kt * T + (tt + 1) * 128],
                                                   qkvw[:, kt * 128:(kt + 1) * 128],
                                                   start=(kt == 0 and tt == tg * 4),
                                                   stop=(kt == 7))
                            sT.inc(ins)
                            bwr(b, sT)

                        @blk.scalar
                        def _(s, tg=tg, b=b):
                            sT.wait(s)
                            ins = s.activation(out=Vp[:, tg * 512:tg * 512 + 512],
                                               in_=bank(b, 512), func=AF.Copy, scale=WSC)
                            sS.inc(ins)
                            brd(b, sS)
                    qkv_tok[(hp + 1, 0)] = sT.n

            @blk.vector
            def _(v, hp=hp):
                sS.wait(v)
                n0, n1 = 2 * hp, 2 * hp + 1
                v.tensor_scalar(smalls[:, 254:255], smalls[:, SM_QB + n0:SM_QB + n0 + 1],
                                HSC, None, ALU.mult)
                vstep(v, v.tensor_scalar(smalls[:, 255:256],
                                         smalls[:, SM_QB + n1:SM_QB + n1 + 1],
                                         HSC, None, ALU.mult))
                vstep(v, v.tensor_copy(smalls[64:128, 254:255], smalls[64:128, 255:256]))
                ins = v.tensor_scalar(Qp[:], Qp[:], smalls[:, 254:255], None, ALU.add)
                sV.inc(ins)

            @blk.gpsimd
            def _(g, hp=hp):
                sT.wait(g, qkv_tok.get(("e", hp), 0))
                g.dma_start(out=posw,
                            in_=bass.AP(din["pos_f"].tensor, hp * 128,
                                        [[1024, 128], [128 * 1024, 2], [1, 128]])).then_inc(sP.h, 16)
                sP.n += 16

            for ch in range(5):
                b = 4 + ch % 2

                @blk.tensor
                def _(t, hp=hp, ch=ch, b=b):
                    sP.wait(t)
                    sV.wait(t)
                    wait_bank_free(t, b)
                    t.matmul(bank(b, 512), posw[:, 0:128],
                             sins0[:, ch * 512:ch * 512 + 512], start=True, stop=False)
                    ins = t.matmul(bank(b, 512), posw[:, 128:256],
                                   sins1[:, ch * 512:ch * 512 + 512], start=False, stop=True)
                    sT.inc(ins)
                    bwr(b, sT)

                @blk.scalar
                def _(s, ch=ch, b=b):
                    sT.wait(s)
                    ins = s.activation(out=emb[:, ch * 512:ch * 512 + 512], in_=bank(b, 512),
                                       func=AF.Copy)
                    sS.inc(ins)
                    brd(b, sS)
            qkv_tok[("e", hp + 1)] = sT.n

            sqb = bf(A_SP6 + 256, 512)

            def norms_max(src, nch, hh0, out_off):
                for ch in range(nch):
                    b = 6 + ch % 2

                    @blk.scalar
                    def _(s, src=src, ch=ch, hh0=hh0):
                        sT.wait(s)
                        ins = s.activation(out=sqb[hh0:hh0 + 64, :],
                                           in_=src[hh0:hh0 + 64, ch * 512:(ch + 1) * 512],
                                           func=AF.Square)
                        sS.inc(ins)

                    @blk.tensor
                    def _(t, ch=ch, b=b, hh0=hh0):
                        sS.wait(t)
                        wait_bank_free(t, b)
                        ins = t.matmul(bank(b, 512, rows=1), ones_b[hh0:hh0 + 64, 0:1],
                                       sqb[hh0:hh0 + 64, :], start=True, stop=True)
                        sT.inc(ins)
                        bwr(b, sT)

                    @blk.vector
                    def _(v, ch=ch, b=b, out_off=out_off):
                        sT.wait(v)
                        vstep(v, v.reduce_max(rowbuf[:, RB_Q + 8:RB_Q + 9],
                                              bank(b, 512, rows=1), axis=AX))
                        if ch == 0:
                            ins = v.tensor_copy(rowbuf[:, out_off:out_off + 1],
                                                rowbuf[:, RB_Q + 8:RB_Q + 9])
                        else:
                            ins = v.tensor_max(rowbuf[:, out_off:out_off + 1],
                                               rowbuf[:, out_off:out_off + 1],
                                               rowbuf[:, RB_Q + 8:RB_Q + 9])
                        vstep(v, ins)
                        brd(b, sV)

            for hi in range(2):
                hh0 = hi * 64
                norms_max(Qp, 4, hh0, RB_Q + 0)
                norms_max(Kp, 4, hh0, RB_Q + 1)
                norms_max(emb, 5, hh0, RB_Q + 2)

                @blk.vector
                def _(v, hi=hi):
                    sV.wait(v)
                    vstep(v, v.tensor_add(rowbuf[:, RB_Q + 3:RB_Q + 4],
                                          rowbuf[:, RB_Q + 1:RB_Q + 2],
                                          rowbuf[:, RB_Q + 2:RB_Q + 3]))
                    vstep(v, v.scalar_tensor_tensor(rowbuf[:, RB_Q + 3:RB_Q + 4],
                                                    rowbuf[:, RB_Q + 3:RB_Q + 4], 0.5,
                                                    rowbuf[:, RB_Q + 0:RB_Q + 1],
                                                    ALU.mult, ALU.add))
                    vstep(v, v.tensor_scalar(ncb[0:1, hi:hi + 1],
                                             rowbuf[:, RB_Q + 3:RB_Q + 4],
                                             -1.0, -1.0, ALU.mult, ALU.add))

                @blk.tensor
                def _(t, hi=hi):
                    sV.wait(t)
                    wait_bank_free(t, 6)
                    ins = t.matmul(bank(6, 1, rows=128), ones_b[0:1, :], ncb[0:1, hi:hi + 1],
                                   start=True, stop=True)
                    sT.inc(ins)
                    bwr(6, sT)

                @blk.vector
                def _(v, hi=hi):
                    sT.wait(v)
                    ins = v.tensor_copy(negc[hi], bank(6, 1, rows=128))
                    sV.inc(ins)
                    brd(6, sV)

            for hi in range(2 if _PH6 >= 2 else 0):
                h = 2 * hp + hi
                hh0 = hi * 64
                tp = None if hi == 0 else (0, 64)

                for qt in range(16):
                    q0 = qt * 128
                    ch0 = max(0, (1920 - q0) // 512)
                    for ch in range(ch0, 4):
                        b = ch % 2

                        @blk.tensor
                        def _(t, qt=qt, ch=ch, b=b, hh0=hh0):
                            sV.wait(t)
                            sS.wait(t)
                            wait_bank_free(t, b)
                            ins = t.matmul(bank(b, 512), Qp[hh0:hh0 + 64, qt * 128:(qt + 1) * 128],
                                           emb[hh0:hh0 + 64, ch * 512:ch * 512 + 512],
                                           start=True, stop=True)
                            sT.inc(ins)
                            bwr(b, sT)

                        @blk.scalar
                        def _(s, ch=ch, b=b):
                            sT.wait(s)
                            sD.wait(s)
                            ins = s.activation(out=posb[:, ch * 512:ch * 512 + 512],
                                               in_=bank(b, 512), func=AF.Copy)
                            sS.inc(ins)
                            brd(b, sS)

                    @blk.sync
                    def _(sy, qt=qt, ch0=ch0):
                        sS.wait(sy)
                        sy.dma_start(out=scr_pos[qt * 128:(qt + 1) * 128, ch0 * 512:T],
                                     in_=posb[:, ch0 * 512:T]).then_inc(sD.h, 16)
                        sD.n += 16
                    pos_tok[(hi, qt)] = sD.n

                for qt in range(16 if _PH6 >= 3 else 0):
                    q0 = qt * 128
                    nkc = qt // 4 + 1
                    for kc in range(nkc):
                        @blk.sync
                        def _(sy, qt=qt, kc=kc, q0=q0, ptok=None):
                            sy.wait_ge(sD.h, pos_tok[(hi, qt)])
                            sV.wait(sy, rel_tok.get(kc % 2, 0))
                            off0 = q0 * POSW + (2047 - q0 + kc * 512)
                            sy.dma_start(out=relb[kc % 2],
                                         in_=bass.AP(scr_pos.tensor, off0,
                                                     [[POSW - 1, 128], [1, 512]])).then_inc(sD.h, 16)
                            sD.n += 16
                        rel_dma_tok[(qt, kc)] = sD.n

                        b = 2 + kc % 2

                        @blk.tensor
                        def _(t, qt=qt, kc=kc, b=b, hh0=hh0):
                            sS.wait(t)
                            wait_bank_free(t, b)
                            ins = t.matmul(bank(b, 512), Qp[hh0:hh0 + 64, qt * 128:(qt + 1) * 128],
                                           Kp[hh0:hh0 + 64, kc * 512:kc * 512 + 512],
                                           start=True, stop=True)
                            sT.inc(ins)
                            bwr(b, sT)

                        @blk.vector
                        def _(v, qt=qt, kc=kc, b=b):
                            sT.wait(v)
                            sS.wait(v)
                            sD.wait(v, rel_dma_tok[(qt, kc)])
                            ins = v.tensor_add(Sf[kc % 2], bank(b, 512), relb[kc % 2])
                            if kc == qt // 4:
                                vv = qt % 4
                                ins = v.tensor_add(Sf[kc % 2], Sf[kc % 2],
                                                   mask_t[:, vv * 512:vv * 512 + 512])
                            sV.inc(ins)
                            brd(b, sV)
                            rel_tok[kc % 2] = sV.n

                        @blk.scalar
                        def _(s, qt=qt, kc=kc, hi=hi):
                            sV.wait(s)
                            sT.wait(s)
                            ins = s.activation(out=Pb[:, kc * 512:kc * 512 + 512],
                                               in_=Sf[kc % 2], func=AF.Exp, bias=negc[hi],
                                               accum_out=rsums[:, kc:kc + 1])
                            sS.inc(ins)

                    @blk.vector
                    def _(v, qt=qt, nkc=nkc):
                        sS.wait(v)
                        sT.wait(v)
                        vstep(v, v.reduce_sum(rsums[:, 6:7], rsums[:, 0:nkc], axis=AX))
                        vstep(v, v.reciprocal(rsums[:, 7:8], rsums[:, 6:7]))
                        ins = v.tensor_scalar(diag[:], ident_b[:], rsums[:, 7:8], None, ALU.mult)
                        sV.inc(ins)

                    ntile = qt + 1
                    for mg in range((ntile + 3) // 4):
                        b = 2 + mg % 2
                        mlo = mg * 4
                        mhi = min(mlo + 4, ntile)

                        @blk.tensor
                        def _(t, qt=qt, b=b, mlo=mlo, mhi=mhi):
                            sV.wait(t)
                            wait_bank_free(t, b)
                            for m in range(mlo, mhi):
                                ins = t.matmul(bank(b, 128, col0=(m - mlo) * 128),
                                               Pb[:, m * 128:(m + 1) * 128], diag[:],
                                               start=True, stop=True)
                            sT.inc(ins)
                            bwr(b, sT)

                        @blk.scalar
                        def _(s, b=b, mlo=mlo, mhi=mhi):
                            sT.wait(s)
                            ins = s.activation(out=PTb[:, mlo * 128:mhi * 128],
                                               in_=bank(b, (mhi - mlo) * 128), func=AF.Copy)
                            sS.inc(ins)
                            brd(b, sS)

                    @blk.tensor
                    def _(t, qt=qt, hh0=hh0, tp=tp, ntile=ntile):
                        sS.wait(t)
                        if qt % 4 == 0:
                            wait_bank_free(t, 7)
                        for m in range(ntile):
                            ins = t.matmul(bank(7, 128, row0=hh0, rows=64, col0=(qt % 4) * 128),
                                           Vp[:, m * 128 + hh0:m * 128 + hh0 + 64],
                                           PTb[:, m * 128:(m + 1) * 128],
                                           start=(m == 0), stop=(m == ntile - 1),
                                           tile_position=tp)
                        sT.inc(ins)
                        if qt % 4 == 3:
                            bwr(7, sT)

                    if qt % 4 == 3:
                        @blk.scalar
                        def _(s, qt=qt, hp=hp, hh0=hh0):
                            sT.wait(s)
                            ins = s.activation(out=ohs[hh0:hh0 + 64, hp * T + (qt - 3) * 128:hp * T + (qt + 1) * 128],
                                               in_=bank(7, 512, row0=hh0, rows=64), func=AF.Copy)
                            sS.inc(ins)
                            brd(7, sS)

        @blk.gpsimd
        def _(g):
            g.wait_ge(sT.h, max(qkv_tok.get((8, 0), 0), qkv_tok.get(("e", 8), 0)))
            if _PH6 >= 4:
                for half in range(2):
                    g.dma_start(out=bf(A_WS6 + half * 8192, 4 * 1024),
                                in_=bass.AP(din["out_k"].tensor, half * 4 * 128 * D,
                                            [[D, 128], [128 * D, 4], [1, D]])).then_inc(sG.h, 16)
                    sG.n += 16
        oktok = sG.n

        outk = bf(A_WS6, 8 * 1024)
        for dt in range(8 if _PH6 >= 4 else 0):
            for ch in range(4):
                it = dt * 4 + ch
                bA = it % 2
                evac = f32(A_SP6 + 256 + (it % 2) * 2048, 512)

                @blk.tensor
                def _(t, dt=dt, ch=ch, bA=bA):
                    if dt == 0 and ch == 0:
                        sG.wait(t, oktok)
                        sS.wait(t)
                    wait_bank_free(t, bA)
                    for hp in range(8):
                        ins = t.matmul(bank(bA, 512), outk[:, hp * 1024 + dt * 128:hp * 1024 + (dt + 1) * 128],
                                       ohs[:, hp * T + ch * 512:hp * T + ch * 512 + 512],
                                       start=(hp == 0), stop=(hp == 7))
                    sT.inc(ins)
                    bwr(bA, sT)

                @blk.vector
                def _(v, dt=dt, ch=ch, bA=bA, it=it, evac=evac):
                    sT.wait(v)
                    if proj_tok.get(it - 2) is not None:
                        sG.wait(v, proj_tok[it - 2])
                    ob = smalls[:, SM_OB + dt:SM_OB + dt + 1]
                    ins = v.tensor_scalar(evac, bank(bA, 512),
                                          OKSC, ob, ALU.mult, ALU.add)
                    sV.inc(ins)
                    brd(bA, sV)

                @blk.gpsimd
                def _(g, dt=dt, ch=ch, it=it, evac=evac):
                    sV.wait(g)
                    g.dma_start(out=scr_xres[dt * 128:(dt + 1) * 128, ch * 512:(ch + 1) * 512],
                                in_=evac, accum_op=ALU.add).then_inc(sG.h, 16)
                    sG.n += 16
                    proj_tok[it] = sG.n

    qkv_tok = {}
    rel_tok = {}
    rel_dma_tok = {}
    proj_tok = {}
    pos_tok = {}

    def ph7():
        barrier()
        xn1 = bf(A_XNM, 8 * T)
        xhat = bf(A_XHAT, 8 * T)
        logit = f32(A_LOG, 2048)
        ltt = f32(A_TMP7, 16)
        routw = bf(A_TMP7 + 2048, 8 * NE)

        @blk.gpsimd
        def _(g):
            g.dma_start(out=routw, in_=bass.AP(din["router"].tensor, 0,
                                               [[NE, 128], [128 * NE, 8], [1, NE]])).then_inc(sG.h, 16)
            sG.n += 16

        for ch in range(4):
            b = ch % 2

            @blk.tensor
            def _(t, ch=ch, b=b):
                sG.wait(t)
                wait_bank_free(t, b)
                for kt in range(8):
                    ins = t.matmul(bank(b, 512, rows=NE), routw[:, kt * NE:(kt + 1) * NE],
                                   xn1[:, kt * T + ch * 512:kt * T + ch * 512 + 512],
                                   start=(kt == 0), stop=(kt == 7))
                sT.inc(ins)
                bwr(b, sT)

            @blk.vector
            def _(v, ch=ch, b=b):
                sT.wait(v)
                ins = v.tensor_copy(AFp[0:NE, (A_LOG // 4) + ch * 512:(A_LOG // 4) + ch * 512 + 512],
                                    bank(b, 512, rows=NE))
                sV.inc(ins)
                brd(b, sV)

        @blk.vector
        def _(v):
            nbt = f32(A_TMP7 + 5760, 16)[:, 0:NE]
            vstep(v, v.memset(nbt, NEG))

        for tt in range(16):
            b = 2 + tt % 2

            @blk.tensor
            def _(t, tt=tt, b=b):
                sV.wait(t)
                wait_bank_free(t, b)
                ins = t.transpose(bank(b, NE, rows=128),
                                  AFp[0:NE, (A_LOG // 4) + tt * 128:(A_LOG // 4) + (tt + 1) * 128],
                                  ident_f[0:NE, 0:NE])
                sT.inc(ins)
                bwr(b, sT)

            @blk.vector
            def _(v, tt=tt, b=b):
                sT.wait(v)
                sS.wait(v)
                lt = f32(A_TMP7 + 4096, 16)[:, 0:NE]
                m1n = f32(A_TMP7 + 4224, 1)
                m2n = f32(A_TMP7 + 4352, 1)
                eq1 = f32(A_TMP7 + 4480, 16)[:, 0:NE]
                eq2 = f32(A_TMP7 + 4608, 16)[:, 0:NE]
                m1p = f32(A_TMP7 + 5376, 1)
                m2p = f32(A_TMP7 + 5504, 1)
                tq = f32(A_TMP7 + 5632, 16)[:, 0:NE]
                nbt = f32(A_TMP7 + 5760, 16)[:, 0:NE]
                vstep(v, v.tensor_copy(lt, bank(b, NE, rows=128)))
                brd(b, sV)
                vstep(v, v.reduce_max(m1n, lt, axis=AX, negate=True))
                vstep(v, v.tensor_scalar(m1p, m1n, -1.0, None, ALU.mult))
                vstep(v, v.tensor_scalar(eq1, lt, m1p, None, ALU.is_ge))
                vstep(v, v.tensor_mul(tq, eq1, nbt))
                vstep(v, v.tensor_add(tq, tq, lt))
                vstep(v, v.reduce_max(m2n, tq, axis=AX, negate=True))
                vstep(v, v.tensor_scalar(m2p, m2n, -1.0, None, ALU.mult))
                vstep(v, v.tensor_scalar(eq2, tq, m2p, None, ALU.is_ge))

            @blk.scalar
            def _(s, tt=tt):
                sV.wait(s)
                lt = f32(A_TMP7 + 4096, 16)[:, 0:NE]
                m1n = f32(A_TMP7 + 4224, 1)
                m2n = f32(A_TMP7 + 4352, 1)
                e2 = f32(A_TMP7 + 4864, 1)
                zz = f32(A_TMP7 + 4992, 1)
                es = f32(A_TMP7 + 5120, 16)[:, 0:NE]
                s.activation(out=es, in_=lt, func=AF.Exp, bias=m1n, accum_out=zz)
                ins = s.activation(out=e2, in_=m2n, func=AF.Exp, scale=-1.0, bias=m1n)
                sS.inc(ins)

            @blk.vector
            def _(v, tt=tt):
                sS.wait(v)
                e2 = f32(A_TMP7 + 4864, 1)
                zz = f32(A_TMP7 + 4992, 1)
                dd = f32(A_TMP7 + 5248, 1)
                eq1 = f32(A_TMP7 + 4480, 16)[:, 0:NE]
                eq2 = f32(A_TMP7 + 4608, 16)[:, 0:NE]
                vstep(v, v.scalar_tensor_tensor(dd, zz, 1e-9, e2, ALU.mult, ALU.add))
                vstep(v, v.tensor_scalar(dd, dd, 1.0, None, ALU.add))
                vstep(v, v.reciprocal(dd, dd))
                vstep(v, v.tensor_mul(e2, e2, dd))
                vstep(v, v.tensor_scalar(eq1, eq1, dd, None, ALU.mult))
                vstep(v, v.tensor_scalar(eq2, eq2, e2, None, ALU.mult))
                vstep(v, v.tensor_add(getok[:, tt * NE:(tt + 1) * NE], eq1, eq2))

        ln_stats(lambda ft, c, w: xn1[:, ft * T + c:ft * T + c + w], 8, RMS_EPS, False, A_TMP7 + 6144,
                 do_mu=False)
        ln_apply(lambda ft, c, w: xn1[:, ft * T + c:ft * T + c + w],
                 lambda ft, c, w: xhat[:, ft * T + c:ft * T + c + w],
                 8, A_TMP7 + 8192, sub_mu=False)

        @blk.vector
        def _(v):
            xb0 = bf(A_XNM, 8 * T)
            for ft in range(8):
                ins = v.tensor_copy(xb0[:, ft * T:(ft + 1) * T], xhat[:, ft * T:(ft + 1) * T])
            sV.inc(ins)

    def ph8():
        barrier()
        xres = f32(A_XRES8, 8 * T)
        tr = f32(A_TR8, 512)
        add_mo = stage == "l0" and _PH4 >= 4
        reload_xres(xres, A_TR8 + 4096, add_mo, False)

        for ft in range(8):
            for tg in range(4):
                b = tg % 2

                @blk.tensor
                def _(t, ft=ft, tg=tg, b=b):
                    sD.wait(t)
                    wait_bank_free(t, b)
                    for j in range(4):
                        tt = tg * 4 + j
                        ins = t.transpose(bank(b, 128, col0=j * 128),
                                          xres[:, ft * T + tt * 128:ft * T + (tt + 1) * 128],
                                          ident_f[:])
                    sT.inc(ins)
                    bwr(b, sT)

                @blk.vector
                def _(v, ft=ft, tg=tg, b=b):
                    sT.wait(v)
                    sD.wait(v)
                    ins = v.tensor_copy(tr, bank(b, 512))
                    sV.inc(ins)
                    brd(b, sV)

                @blk.sync
                def _(sy, ft=ft, tg=tg):
                    sV.wait(sy)
                    sy.dma_start(out=bass.AP(out_t.tensor, (tg * 4) * 128 * D + ft * 128,
                                             [[D, 128], [128 * D, 4], [1, 128]]),
                                 in_=tr).then_inc(sD.h, 16)
                    sD.n += 16

    tr_tok = {}

    def ph9():
        barrier()
        xhat = bf(A_XNM, 8 * T)
        s1 = bf(A_S1, 16 * 512)
        w1b = [bf(A_W1, 8 * HID), bf(A_W1 + 32 * KB, 8 * HID)]
        w2b = bf(A_W2, 16 * 1024)
        gtmp = f32(A_TMP9, 512)

        experts = [("r", e) for e in range(NE)] + [("s", 0), ("s", 1)]

        @blk.gpsimd
        def _(g):
            g.dma_start(out=w1b[0], in_=bass.AP(din["w1"].tensor, 0,
                                                [[HID, 128], [128 * HID, 8], [1, HID]])).then_inc(sG.h, 16)
            sG.n += 16

        for ei, (kind, e) in enumerate(experts):
            par = ei % 2
            @blk.vector
            def _(v, ei=ei, par=par, kind=kind, e=e):
                sG.wait(v)
                for kt in range(8):
                    c = (SM_RMS + kt * NE + e) if kind == "r" else (SM_RMSS + kt * 2 + e)
                    ins = v.tensor_scalar(w1b[par][:, kt * HID:(kt + 1) * HID],
                                          w1b[par][:, kt * HID:(kt + 1) * HID],
                                          smalls[:, c:c + 1], None, ALU.mult)
                sV.inc(ins)
            w1rdy = sV.n

            @blk.gpsimd
            def _(g, ei=ei, kind=kind, e=e):
                if ei > 0:
                    sT.wait(g, ot_tok.get(ei - 1, 0))
                if kind == "r":
                    src = bass.AP(din["w2"].tensor, e * HID * D,
                                  [[D, 128], [128 * D, 16], [1, D]])
                else:
                    src = bass.AP(din["w2s"].tensor, e * HID * D,
                                  [[D, 128], [128 * D, 16], [1, D]])
                g.dma_start(out=w2b, in_=src).then_inc(sW.h, 16)
                sW.n += 16
            w2rdy = sW.n

            for tch in range(4):
                tb0 = tch * 512
                for mt in range(16):
                    b = mt % 2

                    @blk.tensor
                    def _(t, ei=ei, par=par, tch=tch, mt=mt, b=b, w1rdy=w1rdy):
                        if tch == 0 and mt == 0:
                            sV.wait(t, w1rdy)
                            if ei == 0:
                                sV.wait(t)
                        wait_bank_free(t, b)
                        for kt in range(8):
                            ins = t.matmul(bank(b, 512), w1b[par][:, kt * HID + mt * 128:kt * HID + (mt + 1) * 128],
                                           xhat[:, kt * T + tb0:kt * T + tb0 + 512],
                                           start=(kt == 0), stop=(kt == 7))
                        sT.inc(ins)
                        bwr(b, sT)

                    @blk.scalar
                    def _(s, mt=mt, b=b):
                        sT.wait(s)
                        ins = s.activation(out=s1[:, mt * 512:(mt + 1) * 512], in_=bank(b, 512),
                                           func=AF.Silu)
                        sS.inc(ins)
                        brd(b, sS)

                for tt in range(4):
                    ttg = tch * 4 + tt
                    for nd in range(2):
                        b = 2 + (tt * 2 + nd) % 2

                        @blk.tensor
                        def _(t, ei=ei, tch=tch, tt=tt, nd=nd, b=b, w2rdy=w2rdy):
                            sS.wait(t)
                            sW.wait(t, w2rdy)
                            wait_bank_free(t, b)
                            for kt in range(16):
                                ins = t.matmul(bank(b, 512),
                                               s1[:, kt * 512 + tt * 128:kt * 512 + (tt + 1) * 128],
                                               w2b[:, kt * 1024 + nd * 512:kt * 1024 + (nd + 1) * 512],
                                               start=(kt == 0), stop=(kt == 15))
                            sT.inc(ins)
                            bwr(b, sT)
                        if tch == 3 and tt == 3 and nd == 1:
                            ot_tok[ei] = sT.n

                        @blk.vector
                        def _(v, ei=ei, kind=kind, e=e, ttg=ttg, nd=nd, b=b):
                            sT.wait(v)
                            sG.wait(v)
                            if kind == "r":
                                gcol = getok[:, ttg * NE + e:ttg * NE + e + 1]
                                ins = v.tensor_scalar(gtmp, bank(b, 512), gcol, None, ALU.mult)
                            else:
                                ins = v.tensor_copy(gtmp, bank(b, 512))
                            sV.inc(ins)
                            brd(b, sV)

                        @blk.gpsimd
                        def _(g, ttg=ttg, nd=nd):
                            sV.wait(g)
                            g.dma_start(out=out_t[ttg * 128:(ttg + 1) * 128, nd * 512:(nd + 1) * 512],
                                        in_=gtmp, accum_op=ALU.add).then_inc(sG.h, 16)
                            sG.n += 16

                if tch == 3 and ei + 1 < len(experts):
                    nk, ne_ = experts[ei + 1]

                    @blk.gpsimd
                    def _(g, ei=ei, nk=nk, ne_=ne_, par=par):
                        sT.wait(g)
                        if nk == "r":
                            src = bass.AP(din["w1"].tensor, ne_ * D * HID,
                                          [[HID, 128], [128 * HID, 8], [1, HID]])
                        else:
                            src = bass.AP(din["w1s"].tensor, ne_ * D * HID,
                                          [[HID, 128], [128 * HID, 8], [1, HID]])
                        g.dma_start(out=w1b[1 - par], in_=src).then_inc(sG.h, 16)
                        sG.n += 16

    ot_tok = {}
    moe_tok = {}

    def ph_dbg1():
        barrier()
        xb = bf(A_XB, 16 * T)

        @blk.gpsimd
        def _(g):
            g.dma_start(out=dbg[:, 0:32768], in_=xb).then_inc(sG.h, 16)
            sG.n += 16
            g.dma_start(out=dbg[0:16, 32768:36864], in_=bmcm[:]).then_inc(sG.h, 16)
            sG.n += 16
            g.dma_start(out=dbg[:, 36864:37120], in_=aneg_t[:]).then_inc(sG.h, 16)
            sG.n += 16
        barrier()

    run2 = stage != "ln0"
    run3 = stage not in ("ln0", "ffn")
    run4 = run3 and stage not in ("pre", "dbg1")
    run5 = stage in ("dbg4", "dbg5", "attn", "dbg6", "router", "dbg3", "noattn", "moec", "full")
    run6 = stage in ("attn", "dbg6", "full")
    run7 = stage in ("router", "dbg3", "noattn", "full")
    run9 = stage in ("noattn", "moec", "full")

    ph1()
    if run2:
        ph2()
    if run3:
        ph3()
    if stage == "dbg1":
        ph_dbg1()
    if run4:
        ph4()
    if run5:
        ph5()
    if run6:
        ph6()
    if run7:
        ph7()
    if stage == "dbg4":
        barrier()

        @blk.gpsimd
        def _(g):
            g.dma_start(out=dbg[:, 4096:4096 + 16384], in_=bf(A_XNM, 8 * T)).then_inc(sG.h, 16)
            g.dma_start(out=dbg[:, 20480:20480 + 16384], in_=f32(A_XRES1, 8 * T)).then_inc(sG.h, 16)
            sG.n += 32
        barrier()
    if stage == "dbg6":
        barrier()

        @blk.gpsimd
        def _(g):
            g.dma_start(out=dbg[:, 0:16384], in_=bf(A_OH, 8 * T)).then_inc(sG.h, 16)
            g.dma_start(out=dbg[:, 16384:18432], in_=bf(A_QP, T)).then_inc(sG.h, 16)
            g.dma_start(out=dbg[:, 18432:20480], in_=bf(A_KP, T)).then_inc(sG.h, 16)
            g.dma_start(out=dbg[:, 20480:23040], in_=bf(A_EMB, POSW)).then_inc(sG.h, 16)
            g.dma_start(out=dbg[:, 23040:23041], in_=f32(A_SP6 + 64, 1),
                        allow_slow_non_contiguous=True).then_inc(sG.h, 16)
            g.dma_start(out=dbg[:, 23041:23042], in_=f32(A_SP6 + 128, 1),
                        allow_slow_non_contiguous=True).then_inc(sG.h, 16)
            g.dma_start(out=dbg[:, 23042:23050], in_=f32(A_SP6, 8)).then_inc(sG.h, 16)
            g.dma_start(out=dbg[:, 23050:25098], in_=bf(A_VP, 16 * 128)).then_inc(sG.h, 16)
            g.dma_start(out=dbg[:, 24576:40960], in_=bf(A_XNM, 8 * T)).then_inc(sG.h, 16)
            sG.n += 144
        barrier()

    if stage == "dbg3":
        barrier()

        @blk.gpsimd
        def _(g):
            g.dma_start(out=dbg[:, 0:224], in_=getok[:]).then_inc(sG.h, 16)
            g.dma_start(out=dbg[0:NE, 224:2272], in_=f32(A_LOG, 2048)[0:NE, :]).then_inc(sG.h, 16)
            g.dma_start(out=dbg[:, 4096:4096 + 16384], in_=bf(A_XNM, 8 * T)).then_inc(sG.h, 16)
            sG.n += 48
        barrier()
    ph8()
    if run9:
        ph9()

    @blk.sync
    def _(sy):
        sy.wait_ge(sD.h, sD.n)
        sy.wait_ge(sG.h, sG.n)
        sy.wait_ge(sP.h, sP.n)
        sy.wait_ge(sW.h, sW.n)

    stk.close()
    return nc


def kernel(**inputs):
    nc = build(_STAGE)
    in_maps = [stage_inputs(inputs, c) for c in range(8)]
    res = run_bass_kernel_spmd(nc, in_maps, core_ids=list(range(8)))
    out = np.stack([res.results[c]["out_t"] for c in range(8)], axis=0)
    return out.astype(np.float32)


# version 04
# speedup vs baseline: 1.0327x; 1.0327x over previous
import numpy as np
import ml_dtypes
import concourse.bass as bass
import concourse.mybir as mybir
from concourse.bass_utils import run_bass_kernel_spmd
from contextlib import ExitStack

F32 = mybir.dt.float32
BF16 = mybir.dt.bfloat16
AF = mybir.ActivationFunctionType
ALU = mybir.AluOpType
AX = mybir.AxisListType.X

D, T, DI, DS = 1024, 2048, 2048, 16
NH, HS, FQ = 16, 64, 256
HID, NE = 2048, 14
POSW = 2560
LN_EPS = 1e-5
RMS_EPS = 1.1920929e-07
NEG = -1e9
WSC = float((3.0 * D * HS * NH) ** (-0.25))
HSC = float(HS ** (-0.25))
OKSC = float((D * D) ** (-0.25))

_STAGE = "full"
_PH4 = 4
_PH6 = 4

CONSTS = [LN_EPS, RMS_EPS, WSC * HSC, WSC, HSC, 1e-9, NEG, OKSC, 0.5, -0.5,
          2.0, 1.0, -1.0, 0.0, 1.0 / 1024.0, 1.0 / 2048.0]

KB = 1024
A_XNM = 0 * KB
A_XN = 32 * KB
A_XRES = 66 * KB
A_SQ = 130 * KB
A_TMP1 = 140 * KB
A_XK = 34 * KB
A_KACT = 66 * KB
A_WT = 98 * KB
A_WT2 = 114 * KB
A_XNS = 130 * KB
A_TMP2 = 140 * KB
A_XB = 32 * KB
A_WINX = 96 * KB
A_XBR = 128 * KB
A_CT = 138 * KB
A_DELTA = 96 * KB
A_DXB = 112 * KB
A_Y = 120 * KB
A_STRM = 136 * KB
A_SC = 144 * KB
A_XRES1 = 34 * KB
A_SQ1 = 100 * KB
A_TMP5 = 110 * KB
A_OH = 32 * KB
A_QP = 96 * KB
A_KP = 100 * KB
A_VP = 104 * KB
A_EMB = 108 * KB
A_REGA = 113 * KB
A_P = 118 * KB
A_SF = 122 * KB
A_REL = 126 * KB
A_WS6 = 128 * KB
A_SP6 = 144 * KB
A_XHAT = 34 * KB
A_LOG = 66 * KB
A_TMP7 = 76 * KB
A_XRES8 = 66 * KB
A_TR8 = 132 * KB
A_S1 = 32 * KB
A_W1 = 48 * KB
A_W2 = 112 * KB
A_TMP9 = 144 * KB

RB_RS = 0
RB_MU = 2048
RB_SC = 4096
RB_Q = 4608

SM_LNG, SM_LNB, SM_MUR, SM_MUK = 0, 8, 16, 24
SM_CB, SM_BD, SM_DP, SM_OMR, SM_OMK = 32, 48, 64, 80, 88
SM_QB, SM_OB, SM_RMS, SM_RMSS = 96, 112, 120, 232


def _sinusoid(seq_len, freqs, max_period):
    pos = np.arange(seq_len, dtype=np.float64)[:, None]
    idx = np.arange(freqs)[None, :]
    inv_freq = max_period ** (-(idx // 2 * 2) / freqs)
    angle = pos * inv_freq
    return np.where(idx % 2 == 0, np.sin(angle), np.cos(angle)).astype(np.float32)


def host_constants():
    c = {}
    sins = (np.sqrt(2.0) * _sinusoid(T, FQ, 10000).astype(np.float64)
            * (FQ ** (-0.5)) * (HS ** (-0.25))).astype(np.float32)
    cs = np.zeros((FQ, POSW), np.float32)
    cs[:, 0:T] = sins[::-1, :].T
    c["c_sins"] = cs.astype(ml_dtypes.bfloat16)
    c["c_ident_b"] = np.eye(128, dtype=ml_dtypes.bfloat16)
    c["c_ident_f"] = np.eye(128, dtype=np.float32)
    c["c_ones_b"] = np.ones((128, 128), dtype=ml_dtypes.bfloat16)
    c["c_ones_f"] = np.ones((128, 128), dtype=np.float32)
    jj = np.arange(512)[None, None, :]
    pp = np.arange(128)[:, None, None]
    vv = np.arange(4)[None, :, None]
    m = np.where(jj <= vv * 128 + pp, 0.0, NEG).astype(np.float32)
    m = np.broadcast_to(m, (128, 4, 512))
    c["c_mask"] = np.ascontiguousarray(m.reshape(128, 2048)).astype(ml_dtypes.bfloat16)
    c["c_pool"] = np.broadcast_to(np.array(CONSTS, np.float32)[None, :],
                                  (128, len(CONSTS))).copy()
    sel = np.zeros((DS, DS * 128), np.float32)
    for s in range(DS):
        sel[s, s * 128:(s + 1) * 128] = 1.0
    c["c_sel"] = sel.astype(ml_dtypes.bfloat16)
    return c


def stage_inputs(inputs, core):
    m = {}
    m["x_t"] = np.ascontiguousarray(inputs["L0_attn_x"][core].T).astype(np.float32)
    m["w_in"] = inputs["L0_attn_W_in"]
    m["conv_w"] = np.ascontiguousarray(inputs["L0_attn_conv_w"].reshape(DI, 4))
    m["conv_b"] = inputs["L0_attn_conv_b"]
    m["w_b"] = inputs["L0_attn_W_b"]
    m["w_c"] = inputs["L0_attn_W_c"]
    m["w_delta"] = inputs["L0_attn_W_delta"]
    m["b_delta"] = inputs["L0_attn_b_delta"]
    m["log_a"] = inputs["L0_attn_log_A"]
    m["dp"] = inputs["L0_attn_Dp"]
    m["w_out"] = inputs["L0_attn_W_out"]
    m["ln_g"] = inputs["L0_attn_ln_g"]
    m["ln_b"] = inputs["L0_attn_ln_b"]
    m["w_r"] = inputs["L0_ffn_W_r"]
    m["w_k"] = inputs["L0_ffn_W_k"]
    m["w_v"] = inputs["L0_ffn_W_v"]
    m["mu_r"] = inputs["L0_ffn_mu_r"]
    m["mu_k"] = inputs["L0_ffn_mu_k"]
    m["qkv_f"] = np.ascontiguousarray(inputs["L1_attn_qkv"].reshape(D, 3 * NH * HS))
    m["qb_dup"] = np.ascontiguousarray(
        np.concatenate([inputs["L1_attn_q_bias"].T, inputs["L1_attn_q_bias"].T], axis=0))
    m["pos_f"] = np.ascontiguousarray(inputs["L1_attn_positional"].reshape(FQ, NH * HS))
    m["out_k"] = inputs["L1_attn_out_kernel"]
    m["out_b"] = inputs["L1_attn_out_bias"]
    m["router"] = inputs["L1_ffn_router_w"]
    m["rms_t"] = np.ascontiguousarray(inputs["L1_ffn_rms_w"].T)
    m["w1"] = np.ascontiguousarray(inputs["L1_ffn_w1"].reshape(NE * D, HID))
    m["w2"] = np.ascontiguousarray(inputs["L1_ffn_w2"].reshape(NE * HID, D))
    m["rms_s_t"] = np.ascontiguousarray(inputs["L1_ffn_rms_w_s"].T)
    m["w1s"] = np.ascontiguousarray(inputs["L1_ffn_w1_s"].reshape(2 * D, HID))
    m["w2s"] = np.ascontiguousarray(inputs["L1_ffn_w2_s"].reshape(2 * HID, D))
    m.update(host_constants())
    return m


def build(stage):
    nc = bass.Bass("TRN2", target_bir_lowering=False)
    stk = ExitStack()

    def dram(name, shape, dtype=F32, kind="ExternalInput"):
        return nc.dram_tensor(name, shape, dtype, kind=kind).ap()

    din = {}
    f32_in = [("x_t", [D, T]), ("w_in", [D, 2 * DI]), ("conv_w", [DI, 4]),
              ("conv_b", [DI]), ("w_b", [DI, DS]), ("w_c", [DI, DS]),
              ("w_delta", [DI, DI]), ("b_delta", [DI]), ("log_a", [DI, DS]),
              ("dp", [DI]), ("w_out", [DI, D]), ("ln_g", [D]), ("ln_b", [D]),
              ("w_r", [D, D]), ("w_k", [D, D]), ("w_v", [D, D]),
              ("mu_r", [D]), ("mu_k", [D]), ("qkv_f", [D, 3 * NH * HS]),
              ("qb_dup", [128, NH]), ("pos_f", [FQ, NH * HS]),
              ("out_k", [D, D]), ("out_b", [D]), ("router", [D, NE]),
              ("rms_t", [D, NE]), ("w1", [NE * D, HID]), ("w2", [NE * HID, D]),
              ("rms_s_t", [D, 2]), ("w1s", [2 * D, HID]), ("w2s", [2 * HID, D]),
              ("c_ident_f", [128, 128]), ("c_ones_f", [128, 128]),
              ("c_pool", [128, 16])]
    for nm, sh in f32_in:
        din[nm] = dram(nm, sh)
    for nm, sh in [("c_sins", [FQ, POSW]), ("c_ident_b", [128, 128]),
                   ("c_ones_b", [128, 128]), ("c_sel", [DS, DS * 128]),
                   ("c_mask", [128, 2048])]:
        din[nm] = dram(nm, sh, BF16)

    out_t = dram("out_t", [T, D], kind="ExternalOutput")
    dbg = dram("dbg", [128, 40960], kind="ExternalOutput")

    scr_xres = dram("scr_xres", [D, T], kind="Internal")
    scr_xn = dram("scr_xn", [D, T], BF16, kind="Internal")
    scr_mo = dram("scr_mo", [D, T], kind="Internal")
    scr_pos = dram("scr_pos", [T, POSW], BF16, kind="Internal")
    scr_wdelta = dram("scr_wdelta", [DI, DI], BF16, kind="Internal")
    scr_winz = dram("scr_winz", [D, DI], BF16, kind="Internal")
    scr_wout = dram("scr_wout", [DI, D], BF16, kind="Internal")
    scr_wr = dram("scr_wr", [D, D], BF16, kind="Internal")
    scr_wk = dram("scr_wk", [D, D], BF16, kind="Internal")
    scr_wv = dram("scr_wv", [D, D], BF16, kind="Internal")

    ident_b = stk.enter_context(nc.sbuf_tensor("ident_b", [128, 128], BF16))
    ident_f = stk.enter_context(nc.sbuf_tensor("ident_f", [128, 128], F32))
    ones_b = stk.enter_context(nc.sbuf_tensor("ones_b", [128, 128], BF16))
    ones_f = stk.enter_context(nc.sbuf_tensor("ones_f", [128, 128], F32))
    sel_t = stk.enter_context(nc.sbuf_tensor("sel_t", [DS, DS * 128], BF16))
    mask_t = stk.enter_context(nc.sbuf_tensor("mask_t", [128, 2048], BF16))
    sins0 = stk.enter_context(nc.sbuf_tensor("sins0", [128, POSW], BF16))
    sins1 = stk.enter_context(nc.sbuf_tensor("sins1", [128, POSW], BF16))
    smalls = stk.enter_context(nc.sbuf_tensor("smalls", [128, 256], F32))
    convw_t = stk.enter_context(nc.sbuf_tensor("convw_t", [128, 64], F32))
    aneg_t = stk.enter_context(nc.sbuf_tensor("aneg_t", [128, 256], F32))
    hcar_t = stk.enter_context(nc.sbuf_tensor("hcar_t", [128, 256], F32))
    wb_t = stk.enter_context(nc.sbuf_tensor("wb_t", [128, 256], BF16))
    wc_t = stk.enter_context(nc.sbuf_tensor("wc_t", [128, 256], BF16))
    bmcm = stk.enter_context(nc.sbuf_tensor("bmcm", [16, 4096], BF16))
    rowbuf = stk.enter_context(nc.sbuf_tensor("rowbuf", [1, 4800], F32))
    getok = stk.enter_context(nc.sbuf_tensor("getok", [128, 16 * NE], F32))
    cpool = stk.enter_context(nc.sbuf_tensor("cpool", [128, 16], F32))
    arena = stk.enter_context(nc.sbuf_tensor("arena", [128, 76 * KB], BF16))
    AB = arena[:]
    AFp = arena[:].bitcast(F32)

    def bf(off, cols):
        assert off % 2 == 0
        return AB[:, off // 2: off // 2 + cols]

    def f32(off, cols):
        assert off % 4 == 0
        return AFp[:, off // 4: off // 4 + cols]

    PSUM = stk.enter_context(nc.psum_tensor("ps", [128, 4096], F32))

    def bank(b, cols=512, row0=0, rows=128, col0=0):
        return PSUM[row0:row0 + rows, b * 512 + col0:b * 512 + col0 + cols]

    class Sem:
        def __init__(self, name):
            self.h = stk.enter_context(nc.semaphore(name))
            self.n = 0

        def inc(self, instr, k=1):
            instr.then_inc(self.h, k)
            self.n += k

        def wait(self, eng, tok=None):
            eng.wait_ge(self.h, self.n if tok is None else tok)

    sD = Sem("sD")
    sG = Sem("sG")
    sT = Sem("sT")
    sV = Sem("sV")
    sS = Sem("sS")
    sP = Sem("sP")
    sW = Sem("sW")
    sA0 = Sem("sA0")
    sA1 = Sem("sA1")
    SEMS = [sD, sG, sT, sV, sS, sP, sW, sA0, sA1]

    for ci, cv in enumerate(CONSTS):
        nc.const_aps.register(F32, float(cv), cpool[:, ci:ci + 1])

    blk = stk.enter_context(nc.Block())

    def barrier():
        toks = [(s, s.n) for s in SEMS]
        for en in ("sync", "gpsimd", "tensor", "vector", "scalar"):
            def body(e, toks=toks):
                for s, t in toks:
                    if t > 0:
                        e.wait_ge(s.h, t)
            getattr(blk, en)(body)

    def fm_ap(dram_ap, ntiles, width):
        return bass.AP(dram_ap.tensor, 0, [[width, 128], [128 * width, ntiles], [1, width]])

    def col_ap(dram_ap, n_per, ntiles):
        return bass.AP(dram_ap.tensor, 0, [[n_per, 128], [128 * n_per, ntiles], [1, n_per]])

    bk_w = [None] * 8
    bk_r = [None] * 8

    def vstep(v, ins):
        sV.inc(ins)
        v.wait_ge(sV.h, sV.n)

    def bwr(b, sem):
        bk_w[b] = (sem, sem.n)

    def brd(b, sem):
        bk_r[b] = (sem, sem.n)

    def wait_bank_free(eng, b):
        if bk_r[b] is not None:
            s, tok = bk_r[b]
            s.wait(eng, tok)

    def wait_bank_data(eng, b):
        if bk_w[b] is not None:
            s, tok = bk_w[b]
            s.wait(eng, tok)

    @blk.sync
    def _(sy):
        for dst, src in [(ident_b[:], din["c_ident_b"]), (ident_f[:], din["c_ident_f"]),
                         (ones_b[:], din["c_ones_b"]), (ones_f[:], din["c_ones_f"]),
                         (sel_t[:], din["c_sel"]), (mask_t[:], din["c_mask"]),
                         (sins0[:], din["c_sins"][0:128, :]),
                         (sins1[:], din["c_sins"][128:256, :])]:
            sy.dma_start(out=dst, in_=src).then_inc(sD.h, 16)
            sD.n += 16
        for dc, src, npc, nt in [(SM_LNG, din["ln_g"], 1, 8), (SM_LNB, din["ln_b"], 1, 8),
                                 (SM_MUR, din["mu_r"], 1, 8), (SM_MUK, din["mu_k"], 1, 8),
                                 (SM_CB, din["conv_b"], 1, 16), (SM_BD, din["b_delta"], 1, 16),
                                 (SM_DP, din["dp"], 1, 16), (SM_OB, din["out_b"], 1, 8),
                                 (SM_RMS, din["rms_t"], NE, 8), (SM_RMSS, din["rms_s_t"], 2, 8)]:
            sy.dma_start(out=smalls[:, dc:dc + npc * nt], in_=col_ap(src, npc, nt), allow_slow_non_contiguous=True).then_inc(sD.h, 16)
            sD.n += 16
        sy.dma_start(out=smalls[:, SM_QB:SM_QB + NH], in_=din["qb_dup"]).then_inc(sD.h, 16)
        sD.n += 16
        sy.dma_start(out=convw_t[:], in_=col_ap(din["conv_w"], 4, 16), allow_slow_non_contiguous=True).then_inc(sD.h, 16)
        sD.n += 16
        sy.dma_start(out=aneg_t[:], in_=col_ap(din["log_a"], DS, 16), allow_slow_non_contiguous=True).then_inc(sD.h, 16)
        sD.n += 16
        sy.dma_start(out=cpool[:], in_=din["c_pool"]).then_inc(sD.h, 16)
        sD.n += 16
        sy.dma_start(out=f32(A_XRES, 8 * T), in_=fm_ap(din["x_t"], 8, T)).then_inc(sD.h, 16)
        sD.n += 16

    @blk.gpsimd
    def _(g):
        for dst, src in [(scr_wdelta, din["w_delta"]), (scr_winz, din["w_in"][:, DI:2 * DI]),
                         (scr_wout, din["w_out"]), (scr_wr, din["w_r"]),
                         (scr_wk, din["w_k"]), (scr_wv, din["w_v"])]:
            g.dma_start(out=dst, in_=src).then_inc(sG.h, 16)
            sG.n += 16
        g.dma_start(out=wb_t[:], in_=col_ap(din["w_b"], DS, 16)).then_inc(sG.h, 16)
        sG.n += 16
        g.dma_start(out=wc_t[:], in_=col_ap(din["w_c"], DS, 16)).then_inc(sG.h, 16)
        sG.n += 16

    @blk.vector
    def _(v):
        sD.wait(v)
        ins = v.memset(bf(A_TMP1, 512), 0.0)
        sV.inc(ins)

    @blk.sync
    def _(sy):
        sV.wait(sy)
        for rb in range(16):
            sy.dma_start(out=scr_pos[rb * 128:(rb + 1) * 128, T:POSW],
                         in_=bf(A_TMP1, 512)).then_inc(sD.h, 16)
            sD.n += 16

    @blk.scalar
    def _(s):
        sD.wait(s)
        ins = s.activation(out=aneg_t[:], in_=aneg_t[:], func=AF.Exp)
        sS.inc(ins)

    @blk.vector
    def _(v):
        sS.wait(v)
        v.tensor_scalar_mul(aneg_t[:], aneg_t[:], -1.0)
        v.memset(hcar_t[:], 0.0)
        v.tensor_scalar(smalls[:, SM_OMR:SM_OMR + 8], smalls[:, SM_MUR:SM_MUR + 8],
                        -1.0, 1.0, ALU.mult, ALU.add)
        ins = v.tensor_scalar(smalls[:, SM_OMK:SM_OMK + 8], smalls[:, SM_MUK:SM_MUK + 8],
                              -1.0, 1.0, ALU.mult, ALU.add)
        sV.inc(ins)

    def ln_stats(xsl, n_tiles, eps, f32_in, sq_off, do_mu=True):
        onesc = ones_f[:, 0:1] if f32_in else ones_b[:, 0:1]
        sq = [f32(sq_off, 512) if f32_in else bf(sq_off, 512),
              f32(sq_off + 2048, 512) if f32_in else bf(sq_off + 2048, 512)]
        rd_tok = [sT.n, sT.n]
        for ch in range(4):
            b1, b2 = ch, 4 + ch
            for ft in range(n_tiles):
                par = ft % 2

                @blk.scalar
                def _(s, ch=ch, ft=ft, par=par, tok=rd_tok[par]):
                    s.wait_ge(sT.h, tok)
                    if ch == 0 and ft == 0:
                        sD.wait(s)
                        sV.wait(s)
                    ins = s.activation(out=sq[par], in_=xsl(ft, ch * 512, 512), func=AF.Square)
                    sS.inc(ins)

                @blk.tensor
                def _(t, ch=ch, ft=ft, par=par, b1=b1, b2=b2):
                    sS.wait(t)
                    if ft == 0:
                        wait_bank_free(t, b1)
                        wait_bank_free(t, b2)
                    if do_mu:
                        t.matmul(bank(b1, 512, rows=1), onesc, xsl(ft, ch * 512, 512),
                                 start=(ft == 0), stop=(ft == n_tiles - 1))
                    ins = t.matmul(bank(b2, 512, rows=1), onesc, sq[par],
                                   start=(ft == 0), stop=(ft == n_tiles - 1))
                    sT.inc(ins)
                    if ft == n_tiles - 1:
                        bwr(b1, sT)
                        bwr(b2, sT)
                rd_tok[par] = sT.n

        @blk.vector
        def _(v):
            sT.wait(v)
            n = float(n_tiles * 128)
            for ch in range(4):
                b1, b2 = ch, 4 + ch
                if do_mu:
                    v.tensor_copy(rowbuf[:, RB_MU + ch * 512:RB_MU + ch * 512 + 512],
                                  bank(b1, 512, rows=1))
                ins = v.tensor_copy(rowbuf[:, RB_RS + ch * 512:RB_RS + ch * 512 + 512],
                                    bank(b2, 512, rows=1))
            sV.inc(ins)
            for b in range(8):
                brd(b, sV)
            r = rowbuf[:, RB_RS:RB_RS + T]
            mu = rowbuf[:, RB_MU:RB_MU + T]
            ins = v.tensor_scalar_mul(r, r, 1.0 / n)
            if do_mu:
                v.tensor_scalar_mul(mu, mu, 1.0 / n)
                for ch in range(4):
                    sc = rowbuf[:, RB_SC:RB_SC + 512]
                    mch = rowbuf[:, RB_MU + ch * 512:RB_MU + ch * 512 + 512]
                    rch = rowbuf[:, RB_RS + ch * 512:RB_RS + ch * 512 + 512]
                    v.tensor_mul(sc, mch, mch)
                    ins = v.tensor_sub(rch, rch, sc)
            sV.inc(ins)

        @blk.scalar
        def _(s):
            sV.wait(s)
            r = rowbuf[:, RB_RS:RB_RS + T]
            s.activation(out=r, in_=r, func=AF.Ln, bias=float(eps))
            ins = s.activation(out=r, in_=r, func=AF.Exp, scale=-0.5)
            sS.inc(ins)

        @blk.vector
        def _(v):
            sS.wait(v)
            if do_mu:
                ins = v.tensor_mul(rowbuf[:, RB_MU:RB_MU + T], rowbuf[:, RB_MU:RB_MU + T],
                                   rowbuf[:, RB_RS:RB_RS + T])
            else:
                ins = v.tensor_copy(rowbuf[:, RB_SC + 500:RB_SC + 501],
                                    rowbuf[:, RB_RS:RB_RS + 1])
            sV.inc(ins)

    def ln_apply(xsl, dsl, n_tiles, tmp_off, gcol=None, bcol=None, sub_mu=True):
        for ch in range(4):
            b1 = 4 + (ch % 2) * 2
            b2 = b1 + 1

            @blk.tensor
            def _(t, ch=ch, b1=b1, b2=b2):
                sV.wait(t)
                wait_bank_free(t, b1)
                wait_bank_free(t, b2)
                ins = t.matmul(bank(b1, 512), ones_f[0:1, :],
                               rowbuf[:, RB_RS + ch * 512:RB_RS + ch * 512 + 512],
                               start=True, stop=True)
                if sub_mu:
                    ins = t.matmul(bank(b2, 512), ones_f[0:1, :],
                                   rowbuf[:, RB_MU + ch * 512:RB_MU + ch * 512 + 512],
                                   start=True, stop=True)
                sT.inc(ins)
                bwr(b1, sT)
                if sub_mu:
                    bwr(b2, sT)

            @blk.vector
            def _(v, ch=ch, b1=b1, b2=b2):
                sT.wait(v)
                tmp = f32(tmp_off, 512)
                for ft in range(n_tiles):
                    xi = xsl(ft, ch * 512, 512)
                    do = dsl(ft, ch * 512, 512)
                    if gcol is None:
                        if sub_mu:
                            v.tensor_mul(tmp, xi, bank(b1, 512))
                            ins = v.tensor_sub(do, tmp, bank(b2, 512))
                        else:
                            ins = v.tensor_mul(do, xi, bank(b1, 512))
                    else:
                        v.tensor_mul(tmp, xi, bank(b1, 512))
                        v.tensor_sub(tmp, tmp, bank(b2, 512))
                        ins = v.tensor_scalar(do, tmp, gcol[ft], bcol[ft], ALU.mult, ALU.add)
                sV.inc(ins)
                brd(b1, sV)
                brd(b2, sV)

    XN_W = 2052

    def ph1():
        xres = f32(A_XRES, 8 * T)
        xn = bf(A_XN, 8 * XN_W)
        barrier()
        ln_stats(lambda ft, c, w: xres[:, ft * T + c:ft * T + c + w], 8, LN_EPS, True, A_SQ)

        @blk.vector
        def _(v):
            for ft in range(8):
                ins = v.memset(xn[:, ft * XN_W:ft * XN_W + 4], 0.0)
            sV.inc(ins)

        ln_apply(lambda ft, c, w: xres[:, ft * T + c:ft * T + c + w],
                 lambda ft, c, w: xn[:, ft * XN_W + 4 + c:ft * XN_W + 4 + c + w],
                 8, A_TMP1)

        @blk.vector
        def _(v):
            for ft in range(8):
                ins = v.tensor_add(xres[:, ft * T:(ft + 1) * T], xres[:, ft * T:(ft + 1) * T],
                                   xn[:, ft * XN_W + 4:ft * XN_W + 4 + T])
            sV.inc(ins)

        @blk.sync
        def _(sy):
            sV.wait(sy)
            sy.dma_start(out=fm_ap(scr_xres, 8, T), in_=xres).then_inc(sD.h, 16)
            sD.n += 16
            for ft in range(8):
                sy.dma_start(out=scr_xn[ft * 128:(ft + 1) * 128, :],
                             in_=xn[:, ft * XN_W + 4:ft * XN_W + 4 + T]).then_inc(sD.h, 16)
                sD.n += 16

        if stage in ("ln0", "ffn"):
            return
        xnm = bf(A_XNM, 8 * T)
        ln_stats(lambda ft, c, w: xn[:, ft * XN_W + 4 + c:ft * XN_W + 4 + c + w],
                 8, LN_EPS, False, A_SQ)
        gc = [smalls[:, SM_LNG + ft:SM_LNG + ft + 1] for ft in range(8)]
        bc = [smalls[:, SM_LNB + ft:SM_LNB + ft + 1] for ft in range(8)]
        ln_apply(lambda ft, c, w: xn[:, ft * XN_W + 4 + c:ft * XN_W + 4 + c + w],
                 lambda ft, c, w: xnm[:, ft * T + c:ft * T + c + w],
                 8, A_TMP1, gcol=gc, bcol=bc)

    def ph2():
        barrier()
        xk = bf(A_XK, 8 * T)
        kact = bf(A_KACT, 8 * T)
        xstr = [bf(A_XNS, XN_W), bf(A_XNS + 4104, XN_W)]

        @blk.vector
        def _(v):
            v.memset(xstr[0][:, 0:4], 0.0)
            ins = v.memset(xstr[1][:, 0:4], 0.0)
            sV.inc(ins)

        def mix(dst, mcol, omcol):
            toks = {}
            for ft in range(8):
                par = ft % 2

                @blk.sync
                def _(sy, ft=ft, par=par):
                    if ft >= 2:
                        sV.wait(sy, toks[ft - 2])
                    sy.dma_start(out=xstr[par][:, 4:4 + T],
                                 in_=scr_xn[ft * 128:(ft + 1) * 128, :]).then_inc(sD.h, 16)
                    sD.n += 16

                @blk.vector
                def _(v, ft=ft, par=par):
                    sD.wait(v)
                    v.tensor_scalar(dst[:, ft * T:(ft + 1) * T], xstr[par][:, 4:4 + T],
                                    mcol[ft], None, ALU.mult)
                    ins = v.scalar_tensor_tensor(dst[:, ft * T:(ft + 1) * T],
                                                 xstr[par][:, 3:3 + T], omcol[ft],
                                                 dst[:, ft * T:(ft + 1) * T],
                                                 ALU.mult, ALU.add)
                    sV.inc(ins)
                    toks[ft] = sV.n

        mcolr = [smalls[:, SM_MUR + f:SM_MUR + f + 1] for f in range(8)]
        omcolr = [smalls[:, SM_OMR + f:SM_OMR + f + 1] for f in range(8)]
        mcolk = [smalls[:, SM_MUK + f:SM_MUK + f + 1] for f in range(8)]
        omcolk = [smalls[:, SM_OMK + f:SM_OMK + f + 1] for f in range(8)]
        mix(xk, mcolk, omcolk)

        @blk.sync
        def _(sy):
            sG.wait(sy)
            for ct in range(8):
                sy.dma_start(out=bf(A_WT + ct * 2048, D),
                             in_=scr_wk[:, ct * 128:(ct + 1) * 128], transpose=True).then_inc(sD.h, 16)
                sD.n += 16
        wtok = sD.n

        for dt in range(8):
            for ch in range(4):
                b = ch % 2

                @blk.tensor
                def _(t, dt=dt, ch=ch, b=b):
                    if dt == 0 and ch == 0:
                        sD.wait(t, wtok)
                        sV.wait(t)
                    wait_bank_free(t, b)
                    for kt in range(8):
                        ins = t.matmul(bank(b, 512), bf(A_WT + kt * 2048 + dt * 256, 128),
                                       xk[:, kt * T + ch * 512:kt * T + ch * 512 + 512],
                                       start=(kt == 0), stop=(kt == 7))
                    sT.inc(ins)
                    bwr(b, sT)

                @blk.scalar
                def _(s, dt=dt, ch=ch, b=b):
                    sT.wait(s)
                    s.activation(out=bf(A_TMP2, 512), in_=bank(b, 512), func=AF.Relu)
                    ins = s.activation(out=kact[:, dt * T + ch * 512:dt * T + ch * 512 + 512],
                                       in_=bf(A_TMP2, 512), func=AF.Square)
                    sS.inc(ins)
                    brd(b, sS)

        barrier()
        xr = bf(A_XK, 8 * T)
        mix(xr, mcolr, omcolr)

        @blk.sync
        def _(sy):
            for ct in range(8):
                sy.dma_start(out=bf(A_WT + ct * 2048, D),
                             in_=scr_wr[:, ct * 128:(ct + 1) * 128], transpose=True).then_inc(sD.h, 16)
                sD.n += 16
            for ct in range(8):
                sy.dma_start(out=bf(A_WT2 + ct * 2048, D),
                             in_=scr_wv[:, ct * 128:(ct + 1) * 128], transpose=True).then_inc(sD.h, 16)
                sD.n += 16
        wtok2 = sD.n

        for dt in range(8):
            for ch in range(4):
                br = 2 + (ch % 2)
                bv = ch % 2

                @blk.tensor
                def _(t, dt=dt, ch=ch, br=br, bv=bv):
                    if dt == 0 and ch == 0:
                        sD.wait(t, wtok2)
                        sV.wait(t)
                    wait_bank_free(t, br)
                    for kt in range(8):
                        ins = t.matmul(bank(br, 512), bf(A_WT + kt * 2048 + dt * 256, 128),
                                       xr[:, kt * T + ch * 512:kt * T + ch * 512 + 512],
                                       start=(kt == 0), stop=(kt == 7))
                    sT.inc(ins)
                    bwr(br, sT)
                    wait_bank_free(t, bv)
                    for kt in range(8):
                        ins = t.matmul(bank(bv, 512), bf(A_WT2 + kt * 2048 + dt * 256, 128),
                                       kact[:, kt * T + ch * 512:kt * T + ch * 512 + 512],
                                       start=(kt == 0), stop=(kt == 7))
                    sT.inc(ins)
                    bwr(bv, sT)

                @blk.scalar
                def _(s, dt=dt, ch=ch, br=br):
                    sT.wait(s)
                    sV.wait(s)
                    ins = s.activation(out=bf(A_TMP2, 512), in_=bank(br, 512), func=AF.Sigmoid)
                    sS.inc(ins)
                    brd(br, sS)

                @blk.vector
                def _(v, dt=dt, ch=ch, bv=bv):
                    sS.wait(v)
                    sG.wait(v)
                    ins = v.tensor_mul(f32(A_TMP2 + 2048, 512), bank(bv, 512), bf(A_TMP2, 512))
                    sV.inc(ins)
                    brd(bv, sV)

                @blk.gpsimd
                def _(g, dt=dt, ch=ch):
                    sV.wait(g)
                    g.dma_start(out=scr_xres[dt * 128:(dt + 1) * 128, ch * 512:(ch + 1) * 512],
                                in_=f32(A_TMP2 + 2048, 512), accum_op=ALU.add).then_inc(sG.h, 16)
                    sG.n += 16

    def ph3():
        barrier()
        xnm = bf(A_XNM, 8 * T)
        xb = bf(A_XB, 16 * T)
        winx = bf(A_WINX, 8 * T)
        xbr = f32(A_XBR, 2056)
        ct1 = f32(A_CT, T)

        @blk.gpsimd
        def _(g):
            for kt in range(8):
                g.dma_start(out=bf(A_WINX + kt * 4096, DI),
                            in_=din["w_in"][kt * 128:(kt + 1) * 128, 0:DI]).then_inc(sG.h, 16)
                sG.n += 16
        wtok = sG.n

        @blk.vector
        def _(v):
            ins = v.memset(xbr[:, 0:4], 0.0)
            sV.inc(ins)

        for mt in range(16):
            for ch in range(4):
                b = ch % 2

                @blk.tensor
                def _(t, mt=mt, ch=ch, b=b):
                    if mt == 0 and ch == 0:
                        sG.wait(t, wtok)
                    wait_bank_free(t, b)
                    for kt in range(8):
                        ins = t.matmul(bank(b, 512), bf(A_WINX + kt * 4096 + mt * 256, 128),
                                       xnm[:, kt * T + ch * 512:kt * T + ch * 512 + 512],
                                       start=(kt == 0), stop=(kt == 7))
                    sT.inc(ins)
                    bwr(b, sT)

                @blk.vector
                def _(v, mt=mt, ch=ch, b=b):
                    sT.wait(v)
                    sS.wait(v)
                    ins = v.tensor_copy(xbr[:, 4 + ch * 512:4 + ch * 512 + 512], bank(b, 512))
                    sV.inc(ins)
                    brd(b, sV)

            @blk.vector
            def _(v, mt=mt):
                cw = [convw_t[:, mt * 4 + k:mt * 4 + k + 1] for k in range(4)]
                v.tensor_scalar(ct1, xbr[:, 1:1 + T], cw[0], None, ALU.mult)
                for k in (1, 2):
                    v.scalar_tensor_tensor(ct1, xbr[:, 1 + k:1 + k + T], cw[k], ct1,
                                           ALU.mult, ALU.add)
                ins = v.scalar_tensor_tensor(ct1, xbr[:, 4:4 + T], cw[3], ct1,
                                             ALU.mult, ALU.add)
                sV.inc(ins)

            @blk.scalar
            def _(s, mt=mt):
                sV.wait(s)
                ins = s.activation(out=xb[:, mt * T:(mt + 1) * T], in_=ct1, func=AF.Silu,
                                   bias=smalls[:, SM_CB + mt:SM_CB + mt + 1])
                sS.inc(ins)

        for which, wt, coff in ((0, wb_t, 0), (1, wc_t, T)):
            for ch in range(4):
                b = 4 + (which * 2 + ch) % 2

                @blk.tensor
                def _(t, wt=wt, ch=ch, b=b):
                    sS.wait(t)
                    wait_bank_free(t, b)
                    for kt in range(16):
                        ins = t.matmul(bank(b, 512, rows=16), wt[:, kt * 16:(kt + 1) * 16],
                                       xb[:, kt * T + ch * 512:kt * T + ch * 512 + 512],
                                       start=(kt == 0), stop=(kt == 15))
                    sT.inc(ins)
                    bwr(b, sT)

                @blk.vector
                def _(v, ch=ch, b=b, coff=coff):
                    sT.wait(v)
                    ins = v.tensor_copy(bmcm[0:16, coff + ch * 512:coff + ch * 512 + 512],
                                        bank(b, 512, rows=16))
                    sV.inc(ins)
                    brd(b, sV)

    def ph4():
        barrier()
        xnm = bf(A_XNM, 8 * T)
        xb = bf(A_XB, 16 * T)
        TC = 256
        delta = f32(A_DELTA, 16 * TC)
        dxb = bf(A_DXB, 16 * TC)
        yac = f32(A_Y, 16 * TC)
        strm = [bf(A_STRM, 2048), bf(A_STRM + 4096, 2048)]
        bcB = f32(A_SC, TC)
        bcC = f32(A_SC + 1024, TC)
        abar = [f32(A_SC + 2048, TC), f32(A_SC + 3072, TC)]
        bx = f32(A_SC + 4096, TC)
        hbuf = [f32(A_SC + 5120, TC), f32(A_SC + 6144, TC)]
        zs = [bf(A_SC + 6144, TC), bf(A_SC + 6656, TC)]
        mo = [f32(A_SC + 7168, TC), f32(A_SC // 1 + 0, 0)]
        mo = f32(A_SC + 7168, TC)

        exp_tmp = f32(A_DELTA + 16 * TC * 4 - 4096, 0)

        for tc in range(8):
            t0 = tc * TC
            for kt in range(16):
                par = kt % 2

                @blk.sync
                def _(sy, tc=tc, kt=kt, par=par, tok=strm_last[par]):
                    if tc == 0 and kt == 0:
                        sG.wait(sy)
                    sT.wait(sy, tok)
                    sy.dma_start(out=strm[par],
                                 in_=scr_wdelta[kt * 128:(kt + 1) * 128, :]).then_inc(sD.h, 16)
                    sD.n += 16

                @blk.tensor
                def _(t, tc=tc, kt=kt, par=par):
                    sD.wait(t)
                    if kt == 0:
                        for b in range(8):
                            wait_bank_free(t, b)
                    for mt in range(16):
                        ins = t.matmul(bank(mt // 2, TC, col0=(mt % 2) * TC),
                                       strm[par][:, mt * 128:(mt + 1) * 128],
                                       xb[:, kt * T + t0:kt * T + t0 + TC],
                                       start=(kt == 0), stop=(kt == 15))
                    if kt == 0:
                        for mt in range(0, 16, 2):
                            ins = t.matmul(bank(mt // 2, TC, col0=0),
                                           strm[par][:, mt * 128:(mt + 1) * 128],
                                           xb[:, kt * T + t0:kt * T + t0 + TC],
                                           start=False, stop=False)
                    sT.inc(ins)
                strm_last[par] = sT.n
            for b in range(8):
                bwr(b, sT)

            @blk.scalar
            def _(s, tc=tc):
                sT.wait(s)
                sG.wait(s)
                for mt in range(16):
                    bd = smalls[:, SM_BD + mt:SM_BD + mt + 1]
                    s.activation(out=f32(A_SC + 7168, TC), in_=bank(mt // 2, TC, col0=(mt % 2) * TC),
                                 func=AF.Exp, bias=bd)
                    ins = s.activation(out=delta[:, mt * TC:(mt + 1) * TC],
                                       in_=f32(A_SC + 7168, TC), func=AF.Ln, bias=1.0)
                sS.inc(ins)
                for b in range(8):
                    brd(b, sS)

            @blk.vector
            def _(v, tc=tc):
                sS.wait(v)
                for mt in range(16):
                    ins = v.tensor_mul(dxb[:, mt * TC:(mt + 1) * TC],
                                       delta[:, mt * TC:(mt + 1) * TC],
                                       xb[:, mt * T + t0:mt * T + t0 + TC])
                sV.inc(ins)
            if _PH4 < 2:
                continue

            ab_tok = [sV.n, sV.n]
            scan_tok = {}
            for s_i in range(DS):
                @blk.tensor
                def _(t, tc=tc, s_i=s_i):
                    sV.wait(t)
                    wait_bank_free(t, 0)
                    t.matmul(bank(0, TC), sel_t[:, s_i * 128:(s_i + 1) * 128],
                             bmcm[0:16, t0:t0 + TC], start=True, stop=True)
                    ins = t.matmul(bank(0, TC, col0=TC), sel_t[:, s_i * 128:(s_i + 1) * 128],
                                   bmcm[0:16, T + t0:T + t0 + TC], start=True, stop=True)
                    sT.inc(ins)
                    bwr(0, sT)

                @blk.vector
                def _(v, tc=tc, s_i=s_i):
                    sT.wait(v)
                    v.tensor_copy(bcB, bank(0, TC))
                    ins = v.tensor_copy(bcC, bank(0, TC, col0=TC))
                    sV.inc(ins)
                    brd(0, sV)

                for dc in range(16):
                    par = dc % 2
                    k = s_i * 16 + dc

                    @blk.scalar
                    def _(s, tc=tc, s_i=s_i, dc=dc, par=par, k=k, tok=ab_tok[par]):
                        s.wait_ge(sV.h, tok)
                        if k >= 2 and tc < 7:
                            s2, d2 = divmod(k - 2, 16)
                            s.activation(out=hcar_t[:, d2 * 16 + s2:d2 * 16 + s2 + 1],
                                         in_=hbuf[par][:, TC - 1:TC], func=AF.Copy)
                        ins = s.activation(out=abar[par], in_=delta[:, dc * TC:(dc + 1) * TC],
                                           func=AF.Exp,
                                           scale=aneg_t[:, dc * 16 + s_i:dc * 16 + s_i + 1])
                        sS.inc(ins)

                    @blk.vector
                    def _(v, tc=tc, s_i=s_i, dc=dc, par=par):
                        sS.wait(v)
                        v.tensor_mul(bx, dxb[:, dc * TC:(dc + 1) * TC], bcB)
                        v.tensor_tensor_scan(hbuf[par], abar[par], bx,
                                             hcar_t[:, dc * 16 + s_i:dc * 16 + s_i + 1],
                                             ALU.mult, ALU.add)
                        ysl = yac[:, dc * TC:(dc + 1) * TC]
                        if s_i == 0:
                            ins = v.tensor_mul(ysl, hbuf[par], bcC)
                        else:
                            v.tensor_mul(bx, hbuf[par], bcC)
                            ins = v.tensor_add(ysl, ysl, bx)
                        sV.inc(ins)
                    ab_tok[par] = sV.n
                    scan_tok[k] = sV.n

            if tc < 7:
                @blk.scalar
                def _(s, tc=tc, t254=scan_tok[254], t255=scan_tok[255]):
                    s.wait_ge(sV.h, t254)
                    s.activation(out=hcar_t[:, 14 * 16 + 15:14 * 16 + 15 + 1],
                                 in_=hbuf[0][:, TC - 1:TC], func=AF.Copy)
                    s.wait_ge(sV.h, t255)
                    ins = s.activation(out=hcar_t[:, 15 * 16 + 15:15 * 16 + 15 + 1],
                                       in_=hbuf[1][:, TC - 1:TC], func=AF.Copy)
                    sS.inc(ins)

            if _PH4 < 3:
                continue
            @blk.vector
            def _(v, tc=tc):
                for mt in range(16):
                    ins = v.scalar_tensor_tensor(yac[:, mt * TC:(mt + 1) * TC],
                                                 xb[:, mt * T + t0:mt * T + t0 + TC],
                                                 smalls[:, SM_DP + mt:SM_DP + mt + 1],
                                                 yac[:, mt * TC:(mt + 1) * TC],
                                                 ALU.mult, ALU.add)
                sV.inc(ins)

            for kt in range(8):
                par = kt % 2

                @blk.sync
                def _(sy, tc=tc, kt=kt, par=par, tok=strm_last[par]):
                    sT.wait(sy, tok)
                    sy.dma_start(out=strm[par],
                                 in_=scr_winz[kt * 128:(kt + 1) * 128, :]).then_inc(sD.h, 16)
                    sD.n += 16

                @blk.tensor
                def _(t, tc=tc, kt=kt, par=par):
                    sD.wait(t)
                    if kt == 0:
                        sV.wait(t)
                        for b in range(8):
                            wait_bank_free(t, b)
                    for mt in range(16):
                        ins = t.matmul(bank(mt // 2, TC, col0=(mt % 2) * TC),
                                       strm[par][:, mt * 128:(mt + 1) * 128],
                                       xnm[:, kt * T + t0:kt * T + t0 + TC],
                                       start=(kt == 0), stop=(kt == 7))
                    if kt == 0:
                        for mt in range(0, 16, 2):
                            ins = t.matmul(bank(mt // 2, TC, col0=0),
                                           strm[par][:, mt * 128:(mt + 1) * 128],
                                           xnm[:, kt * T + t0:kt * T + t0 + TC],
                                           start=False, stop=False)
                    sT.inc(ins)
                strm_last[par] = sT.n
            for b in range(8):
                bwr(b, sT)

            yg = dxb
            for mt in range(16):
                par = mt % 2

                @blk.scalar
                def _(s, tc=tc, mt=mt, par=par, tok=None):
                    sT.wait(s)
                    if mt >= 2:
                        s.wait_ge(sV.h, yg_tok.get((tc, mt - 2), 0))
                    ins = s.activation(out=zs[par], in_=bank(mt // 2, TC, col0=(mt % 2) * TC),
                                       func=AF.Silu)
                    sS.inc(ins)
                    if mt == 15:
                        for b in range(8):
                            brd(b, sS)

                @blk.vector
                def _(v, tc=tc, mt=mt, par=par):
                    sS.wait(v)
                    ins = v.tensor_mul(yg[:, mt * TC:(mt + 1) * TC],
                                       yac[:, mt * TC:(mt + 1) * TC], zs[par])
                    sV.inc(ins)
                    yg_tok[(tc, mt)] = sV.n

            if stage == "dbg2" and tc == 0:
                @blk.gpsimd
                def _(g):
                    sV.wait(g)
                    sS.wait(g)
                    g.dma_start(out=dbg[:, 0:4096], in_=delta).then_inc(sG.h, 16)
                    g.dma_start(out=dbg[:, 4096:8192], in_=yac).then_inc(sG.h, 16)
                    g.dma_start(out=dbg[:, 8192:12288], in_=yg).then_inc(sG.h, 16)
                    g.dma_start(out=dbg[:, 12288:12544], in_=hcar_t[:]).then_inc(sG.h, 16)
                    sG.n += 64
            if _PH4 < 4:
                continue
            for kt in range(16):
                par = kt % 2

                @blk.sync
                def _(sy, tc=tc, kt=kt, par=par, tok=strm_last[par]):
                    sT.wait(sy, tok)
                    sy.dma_start(out=strm[par][:, 0:1024],
                                 in_=scr_wout[kt * 128:(kt + 1) * 128, :]).then_inc(sD.h, 16)
                    sD.n += 16

                @blk.tensor
                def _(t, tc=tc, kt=kt, par=par):
                    sD.wait(t)
                    if kt == 0:
                        sV.wait(t)
                        for b in range(4):
                            wait_bank_free(t, b)
                    for dt in range(8):
                        ins = t.matmul(bank(dt // 2, TC, col0=(dt % 2) * TC),
                                       strm[par][:, dt * 128:(dt + 1) * 128],
                                       yg[:, kt * TC:(kt + 1) * TC],
                                       start=(kt == 0), stop=(kt == 15))
                    if kt == 0:
                        for dt in range(0, 8, 2):
                            ins = t.matmul(bank(dt // 2, TC, col0=0),
                                           strm[par][:, dt * 128:(dt + 1) * 128],
                                           yg[:, kt * TC:(kt + 1) * TC],
                                           start=False, stop=False)
                    sT.inc(ins)
                strm_last[par] = sT.n
            for b in range(4):
                bwr(b, sT)

            for dt in range(8):
                @blk.vector
                def _(v, tc=tc, dt=dt):
                    sT.wait(v)
                    sG.wait(v)
                    ins = v.tensor_copy(mo, bank(dt // 2, TC, col0=(dt % 2) * TC))
                    sV.inc(ins)
                    if dt == 7:
                        for b in range(4):
                            brd(b, sV)

                @blk.gpsimd
                def _(g, tc=tc, dt=dt):
                    sV.wait(g)
                    g.dma_start(out=scr_mo[dt * 128:(dt + 1) * 128, t0:t0 + TC],
                                in_=mo).then_inc(sG.h, 16)
                    sG.n += 16
                    mo_tok[(tc, dt)] = sG.n

    strm_tok = {}
    strm_last = [0, 0]
    yg_tok = {}
    mo_tok = {}

    def reload_xres(xres, tmp_off, add_mo, respill):
        @blk.sync
        def _(sy):
            sG.wait(sy)
            sy.dma_start(out=xres, in_=fm_ap(scr_xres, 8, T)).then_inc(sD.h, 16)
            sD.n += 16

        if add_mo:
            for ft in range(8):
                @blk.sync
                def _(sy, ft=ft):
                    sV.wait(sy)
                    sy.dma_start(out=f32(tmp_off, T),
                                 in_=scr_mo[ft * 128:(ft + 1) * 128, :]).then_inc(sD.h, 16)
                    sD.n += 16

                @blk.vector
                def _(v, ft=ft):
                    sD.wait(v)
                    ins = v.tensor_add(xres[:, ft * T:(ft + 1) * T],
                                       xres[:, ft * T:(ft + 1) * T], f32(tmp_off, T))
                    sV.inc(ins)

        if respill:
            @blk.sync
            def _(sy):
                sV.wait(sy)
                sy.wait_ge(sD.h, sD.n)
                sy.dma_start(out=fm_ap(scr_xres, 8, T), in_=xres).then_inc(sD.h, 16)
                sD.n += 16

    def ph5():
        barrier()
        xres = f32(A_XRES1, 8 * T)
        xn1 = bf(A_XNM, 8 * T)
        reload_xres(xres, A_SQ1, stage not in ("ln0", "ffn", "pre"), True)
        if stage == "dbg5":
            barrier()

            @blk.gpsimd
            def _(g):
                g.dma_start(out=dbg[:, 20480:20480 + 16384], in_=xres).then_inc(sG.h, 16)
                sG.n += 16
            barrier()
            return

        ln_stats(lambda ft, c, w: xres[:, ft * T + c:ft * T + c + w], 8, LN_EPS, True, A_SQ1)
        ln_apply(lambda ft, c, w: xres[:, ft * T + c:ft * T + c + w],
                 lambda ft, c, w: xn1[:, ft * T + c:ft * T + c + w],
                 8, A_TMP5)

    def ph6():
        barrier()
        xn1 = bf(A_XNM, 8 * T)
        ohs = bf(A_OH, 16 * T)
        Qp = bf(A_QP, T)
        Kp = bf(A_KP, T)
        Vp = bf(A_VP, 16 * 128)
        emb = bf(A_EMB, POSW)
        posb = bf(A_REGA, POSW)
        PTb = bf(A_REGA, T)
        diag = bf(A_REGA + 4096, 128)
        Pb = bf(A_P, T)
        QQ = bf(A_P, T)
        Sf = [f32(A_SF, 512), f32(A_SF + 2048, 512)]
        relb = [bf(A_REL, 512), bf(A_REL + 1024, 512)]
        rsums = f32(A_SP6, 8)
        negc = [f32(A_SP6 + 64, 1), f32(A_SP6 + 128, 1)]
        ncb = bf(A_SP6 + 192, 2)
        qkvw = bf(A_WS6, 8 * 128)
        posw = bf(A_WS6 + 2048, 2 * 128)

        for hp in range(8):
            for a in range(3):
                @blk.gpsimd
                def _(g, hp=hp, a=a):
                    if hp > 0 or a > 0:
                        sT.wait(g, qkv_tok.get((hp, a), 0))
                    src = din["qkv_f"]
                    c0 = a * 1024 + hp * 128
                    g.dma_start(out=qkvw,
                                in_=bass.AP(src.tensor, c0,
                                            [[3072, 128], [128 * 3072, 8], [1, 128]])).then_inc(sG.h, 16)
                    sG.n += 16

                if a < 2:
                    dst = Qp if a == 0 else Kp
                    sc = WSC * HSC
                    for ch in range(4):
                        b = ch % 2

                        @blk.tensor
                        def _(t, hp=hp, a=a, ch=ch, b=b):
                            sG.wait(t)
                            if hp == 0 and a == 0 and ch == 0:
                                sV.wait(t)
                            wait_bank_free(t, b)
                            for kt in range(8):
                                ins = t.matmul(bank(b, 512), qkvw[:, kt * 128:(kt + 1) * 128],
                                               xn1[:, kt * T + ch * 512:kt * T + ch * 512 + 512],
                                               start=(kt == 0), stop=(kt == 7))
                            sT.inc(ins)
                            bwr(b, sT)

                        @blk.scalar
                        def _(s, dst=dst, sc=sc, ch=ch, b=b):
                            sT.wait(s)
                            ins = s.activation(out=dst[:, ch * 512:ch * 512 + 512],
                                               in_=bank(b, 512), func=AF.Copy, scale=sc)
                            sS.inc(ins)
                            brd(b, sS)
                    qkv_tok[(hp, a + 1)] = sT.n
                else:
                    for tg in range(4):
                        b = 2 + tg % 2

                        @blk.tensor
                        def _(t, hp=hp, tg=tg, b=b):
                            sG.wait(t)
                            wait_bank_free(t, b)
                            for kt in range(8):
                                for tt in range(tg * 4, tg * 4 + 4):
                                    ins = t.matmul(bank(b, 128, col0=(tt % 4) * 128),
                                                   xn1[:, kt * T + tt * 128:kt * T + (tt + 1) * 128],
                                                   qkvw[:, kt * 128:(kt + 1) * 128],
                                                   start=(kt == 0 and tt == tg * 4),
                                                   stop=(kt == 7))
                            sT.inc(ins)
                            bwr(b, sT)

                        @blk.scalar
                        def _(s, tg=tg, b=b):
                            sT.wait(s)
                            ins = s.activation(out=Vp[:, tg * 512:tg * 512 + 512],
                                               in_=bank(b, 512), func=AF.Copy, scale=WSC)
                            sS.inc(ins)
                            brd(b, sS)
                    qkv_tok[(hp + 1, 0)] = sT.n

            @blk.vector
            def _(v, hp=hp):
                sS.wait(v)
                n0, n1 = 2 * hp, 2 * hp + 1
                v.tensor_scalar(smalls[:, 254:255], smalls[:, SM_QB + n0:SM_QB + n0 + 1],
                                HSC, None, ALU.mult)
                vstep(v, v.tensor_scalar(smalls[:, 255:256],
                                         smalls[:, SM_QB + n1:SM_QB + n1 + 1],
                                         HSC, None, ALU.mult))
                vstep(v, v.tensor_copy(smalls[64:128, 254:255], smalls[64:128, 255:256]))
                ins = v.tensor_scalar(Qp[:], Qp[:], smalls[:, 254:255], None, ALU.add)
                sV.inc(ins)

            @blk.gpsimd
            def _(g, hp=hp):
                sT.wait(g, qkv_tok.get(("e", hp), 0))
                g.dma_start(out=posw,
                            in_=bass.AP(din["pos_f"].tensor, hp * 128,
                                        [[1024, 128], [128 * 1024, 2], [1, 128]])).then_inc(sP.h, 16)
                sP.n += 16

            for ch in range(5):
                b = 4 + ch % 2

                @blk.tensor
                def _(t, hp=hp, ch=ch, b=b):
                    sP.wait(t)
                    sV.wait(t)
                    wait_bank_free(t, b)
                    t.matmul(bank(b, 512), posw[:, 0:128],
                             sins0[:, ch * 512:ch * 512 + 512], start=True, stop=False)
                    ins = t.matmul(bank(b, 512), posw[:, 128:256],
                                   sins1[:, ch * 512:ch * 512 + 512], start=False, stop=True)
                    sT.inc(ins)
                    bwr(b, sT)

                @blk.scalar
                def _(s, ch=ch, b=b):
                    sT.wait(s)
                    ins = s.activation(out=emb[:, ch * 512:ch * 512 + 512], in_=bank(b, 512),
                                       func=AF.Copy)
                    sS.inc(ins)
                    brd(b, sS)
            qkv_tok[("e", hp + 1)] = sT.n

            sqb = bf(A_SP6 + 256, 512)

            def norms_max(src, nch, hh0, out_off):
                for ch in range(nch):
                    b = 6 + ch % 2

                    @blk.scalar
                    def _(s, src=src, ch=ch, hh0=hh0):
                        sT.wait(s)
                        ins = s.activation(out=sqb[hh0:hh0 + 64, :],
                                           in_=src[hh0:hh0 + 64, ch * 512:(ch + 1) * 512],
                                           func=AF.Square)
                        sS.inc(ins)

                    @blk.tensor
                    def _(t, ch=ch, b=b, hh0=hh0):
                        sS.wait(t)
                        wait_bank_free(t, b)
                        ins = t.matmul(bank(b, 512, rows=1), ones_b[hh0:hh0 + 64, 0:1],
                                       sqb[hh0:hh0 + 64, :], start=True, stop=True)
                        sT.inc(ins)
                        bwr(b, sT)

                    @blk.vector
                    def _(v, ch=ch, b=b, out_off=out_off):
                        sT.wait(v)
                        vstep(v, v.reduce_max(rowbuf[:, RB_Q + 8:RB_Q + 9],
                                              bank(b, 512, rows=1), axis=AX))
                        if ch == 0:
                            ins = v.tensor_copy(rowbuf[:, out_off:out_off + 1],
                                                rowbuf[:, RB_Q + 8:RB_Q + 9])
                        else:
                            ins = v.tensor_max(rowbuf[:, out_off:out_off + 1],
                                               rowbuf[:, out_off:out_off + 1],
                                               rowbuf[:, RB_Q + 8:RB_Q + 9])
                        vstep(v, ins)
                        brd(b, sV)

            for hi in range(2):
                hh0 = hi * 64
                norms_max(Qp, 4, hh0, RB_Q + 0)
                norms_max(Kp, 4, hh0, RB_Q + 1)
                norms_max(emb, 5, hh0, RB_Q + 2)

                @blk.vector
                def _(v, hi=hi):
                    sV.wait(v)
                    vstep(v, v.tensor_add(rowbuf[:, RB_Q + 3:RB_Q + 4],
                                          rowbuf[:, RB_Q + 1:RB_Q + 2],
                                          rowbuf[:, RB_Q + 2:RB_Q + 3]))
                    vstep(v, v.scalar_tensor_tensor(rowbuf[:, RB_Q + 3:RB_Q + 4],
                                                    rowbuf[:, RB_Q + 3:RB_Q + 4], 0.5,
                                                    rowbuf[:, RB_Q + 0:RB_Q + 1],
                                                    ALU.mult, ALU.add))
                    vstep(v, v.tensor_scalar(ncb[0:1, hi:hi + 1],
                                             rowbuf[:, RB_Q + 3:RB_Q + 4],
                                             -1.0, -1.0, ALU.mult, ALU.add))

                @blk.tensor
                def _(t, hi=hi):
                    sV.wait(t)
                    wait_bank_free(t, 6)
                    ins = t.matmul(bank(6, 1, rows=128), ones_b[0:1, :], ncb[0:1, hi:hi + 1],
                                   start=True, stop=True)
                    sT.inc(ins)
                    bwr(6, sT)

                @blk.vector
                def _(v, hi=hi):
                    sT.wait(v)
                    ins = v.tensor_copy(negc[hi], bank(6, 1, rows=128))
                    sV.inc(ins)
                    brd(6, sV)

            for hi in range(2 if _PH6 >= 2 else 0):
                h = 2 * hp + hi
                hh0 = hi * 64
                tp = None if hi == 0 else (0, 64)

                for qt in range(16):
                    q0 = qt * 128
                    ch0 = max(0, (1920 - q0) // 512)
                    for ch in range(ch0, 4):
                        b = ch % 2

                        @blk.tensor
                        def _(t, qt=qt, ch=ch, b=b, hh0=hh0):
                            sV.wait(t)
                            sS.wait(t)
                            wait_bank_free(t, b)
                            ins = t.matmul(bank(b, 512), Qp[hh0:hh0 + 64, qt * 128:(qt + 1) * 128],
                                           emb[hh0:hh0 + 64, ch * 512:ch * 512 + 512],
                                           start=True, stop=True)
                            sT.inc(ins)
                            bwr(b, sT)

                        @blk.scalar
                        def _(s, ch=ch, b=b):
                            sT.wait(s)
                            sD.wait(s)
                            ins = s.activation(out=posb[:, ch * 512:ch * 512 + 512],
                                               in_=bank(b, 512), func=AF.Copy)
                            sS.inc(ins)
                            brd(b, sS)

                    @blk.sync
                    def _(sy, qt=qt, ch0=ch0):
                        sS.wait(sy)
                        sy.dma_start(out=scr_pos[qt * 128:(qt + 1) * 128, ch0 * 512:T],
                                     in_=posb[:, ch0 * 512:T]).then_inc(sD.h, 16)
                        sD.n += 16
                    pos_tok[(hi, qt)] = sD.n

                for qt in range(16 if _PH6 >= 3 else 0):
                    q0 = qt * 128
                    nkc = qt // 4 + 1
                    for kc in range(nkc):
                        @blk.sync
                        def _(sy, qt=qt, kc=kc, q0=q0, ptok=None):
                            sy.wait_ge(sD.h, pos_tok[(hi, qt)])
                            sV.wait(sy, rel_tok.get(kc % 2, 0))
                            off0 = q0 * POSW + (2047 - q0 + kc * 512)
                            sy.dma_start(out=relb[kc % 2],
                                         in_=bass.AP(scr_pos.tensor, off0,
                                                     [[POSW - 1, 128], [1, 512]])).then_inc(sD.h, 16)
                            sD.n += 16
                        rel_dma_tok[(qt, kc)] = sD.n

                        b = 2 + kc % 2

                        @blk.tensor
                        def _(t, qt=qt, kc=kc, b=b, hh0=hh0):
                            sS.wait(t)
                            wait_bank_free(t, b)
                            ins = t.matmul(bank(b, 512), Qp[hh0:hh0 + 64, qt * 128:(qt + 1) * 128],
                                           Kp[hh0:hh0 + 64, kc * 512:kc * 512 + 512],
                                           start=True, stop=True)
                            sT.inc(ins)
                            bwr(b, sT)

                        @blk.vector
                        def _(v, qt=qt, kc=kc, b=b):
                            sT.wait(v)
                            sS.wait(v)
                            sD.wait(v, rel_dma_tok[(qt, kc)])
                            ins = v.tensor_add(Sf[kc % 2], bank(b, 512), relb[kc % 2])
                            if kc == qt // 4:
                                vv = qt % 4
                                ins = v.tensor_add(Sf[kc % 2], Sf[kc % 2],
                                                   mask_t[:, vv * 512:vv * 512 + 512])
                            sV.inc(ins)
                            brd(b, sV)
                            rel_tok[kc % 2] = sV.n

                        @blk.scalar
                        def _(s, qt=qt, kc=kc, hi=hi):
                            sV.wait(s)
                            sT.wait(s)
                            ins = s.activation(out=Pb[:, kc * 512:kc * 512 + 512],
                                               in_=Sf[kc % 2], func=AF.Exp, bias=negc[hi],
                                               accum_out=rsums[:, kc:kc + 1])
                            sS.inc(ins)

                    @blk.vector
                    def _(v, qt=qt, nkc=nkc):
                        sS.wait(v)
                        sT.wait(v)
                        vstep(v, v.reduce_sum(rsums[:, 6:7], rsums[:, 0:nkc], axis=AX))
                        vstep(v, v.reciprocal(rsums[:, 7:8], rsums[:, 6:7]))
                        ins = v.tensor_scalar(diag[:], ident_b[:], rsums[:, 7:8], None, ALU.mult)
                        sV.inc(ins)

                    ntile = qt + 1
                    for mg in range((ntile + 3) // 4):
                        b = 2 + mg % 2
                        mlo = mg * 4
                        mhi = min(mlo + 4, ntile)

                        @blk.tensor
                        def _(t, qt=qt, b=b, mlo=mlo, mhi=mhi):
                            sV.wait(t)
                            wait_bank_free(t, b)
                            for m in range(mlo, mhi):
                                ins = t.matmul(bank(b, 128, col0=(m - mlo) * 128),
                                               Pb[:, m * 128:(m + 1) * 128], diag[:],
                                               start=True, stop=True)
                            sT.inc(ins)
                            bwr(b, sT)

                        @blk.scalar
                        def _(s, b=b, mlo=mlo, mhi=mhi):
                            sT.wait(s)
                            ins = s.activation(out=PTb[:, mlo * 128:mhi * 128],
                                               in_=bank(b, (mhi - mlo) * 128), func=AF.Copy)
                            sS.inc(ins)
                            brd(b, sS)

                    @blk.tensor
                    def _(t, qt=qt, hh0=hh0, tp=tp, ntile=ntile):
                        sS.wait(t)
                        if qt % 4 == 0:
                            wait_bank_free(t, 7)
                        for m in range(ntile):
                            ins = t.matmul(bank(7, 128, row0=hh0, rows=64, col0=(qt % 4) * 128),
                                           Vp[:, m * 128 + hh0:m * 128 + hh0 + 64],
                                           PTb[:, m * 128:(m + 1) * 128],
                                           start=(m == 0), stop=(m == ntile - 1),
                                           tile_position=tp)
                        sT.inc(ins)
                        if qt % 4 == 3:
                            bwr(7, sT)

                    if qt % 4 == 3:
                        @blk.scalar
                        def _(s, qt=qt, hp=hp, hh0=hh0):
                            sT.wait(s)
                            ins = s.activation(out=ohs[hh0:hh0 + 64, hp * T + (qt - 3) * 128:hp * T + (qt + 1) * 128],
                                               in_=bank(7, 512, row0=hh0, rows=64), func=AF.Copy)
                            sS.inc(ins)
                            brd(7, sS)

        @blk.gpsimd
        def _(g):
            g.wait_ge(sT.h, max(qkv_tok.get((8, 0), 0), qkv_tok.get(("e", 8), 0)))
            if _PH6 >= 4:
                for half in range(2):
                    g.dma_start(out=bf(A_WS6 + half * 8192, 4 * 1024),
                                in_=bass.AP(din["out_k"].tensor, half * 4 * 128 * D,
                                            [[D, 128], [128 * D, 4], [1, D]])).then_inc(sG.h, 16)
                    sG.n += 16
        oktok = sG.n

        outk = bf(A_WS6, 8 * 1024)
        for dt in range(8 if _PH6 >= 4 else 0):
            for ch in range(4):
                it = dt * 4 + ch
                bA = it % 2
                evac = f32(A_SP6 + 256 + (it % 2) * 2048, 512)

                @blk.tensor
                def _(t, dt=dt, ch=ch, bA=bA):
                    if dt == 0 and ch == 0:
                        sG.wait(t, oktok)
                        sS.wait(t)
                    wait_bank_free(t, bA)
                    for hp in range(8):
                        ins = t.matmul(bank(bA, 512), outk[:, hp * 1024 + dt * 128:hp * 1024 + (dt + 1) * 128],
                                       ohs[:, hp * T + ch * 512:hp * T + ch * 512 + 512],
                                       start=(hp == 0), stop=(hp == 7))
                    sT.inc(ins)
                    bwr(bA, sT)

                asem = sA0 if it % 2 == 0 else sA1

                @blk.vector
                def _(v, dt=dt, ch=ch, bA=bA, it=it, evac=evac, asem=asem):
                    sT.wait(v)
                    if proj_tok.get(it - 2) is not None:
                        asem.wait(v, proj_tok[it - 2])
                    ob = smalls[:, SM_OB + dt:SM_OB + dt + 1]
                    ins = v.tensor_scalar(evac, bank(bA, 512),
                                          OKSC, ob, ALU.mult, ALU.add)
                    sV.inc(ins)
                    brd(bA, sV)

                @blk.gpsimd
                def _(g, dt=dt, ch=ch, it=it, evac=evac, asem=asem):
                    sV.wait(g)
                    g.dma_start(out=scr_xres[dt * 128:(dt + 1) * 128, ch * 512:(ch + 1) * 512],
                                in_=evac, accum_op=ALU.add).then_inc(asem.h, 16)
                    asem.n += 16
                    proj_tok[it] = asem.n

    qkv_tok = {}
    rel_tok = {}
    rel_dma_tok = {}
    proj_tok = {}
    pos_tok = {}

    def ph7():
        barrier()
        xn1 = bf(A_XNM, 8 * T)
        xhat = bf(A_XHAT, 8 * T)
        logit = f32(A_LOG, 2048)
        ltt = f32(A_TMP7, 16)
        routw = bf(A_TMP7 + 2048, 8 * NE)

        @blk.gpsimd
        def _(g):
            g.dma_start(out=routw, in_=bass.AP(din["router"].tensor, 0,
                                               [[NE, 128], [128 * NE, 8], [1, NE]])).then_inc(sG.h, 16)
            sG.n += 16

        for ch in range(4):
            b = ch % 2

            @blk.tensor
            def _(t, ch=ch, b=b):
                sG.wait(t)
                wait_bank_free(t, b)
                for kt in range(8):
                    ins = t.matmul(bank(b, 512, rows=NE), routw[:, kt * NE:(kt + 1) * NE],
                                   xn1[:, kt * T + ch * 512:kt * T + ch * 512 + 512],
                                   start=(kt == 0), stop=(kt == 7))
                sT.inc(ins)
                bwr(b, sT)

            @blk.vector
            def _(v, ch=ch, b=b):
                sT.wait(v)
                ins = v.tensor_copy(AFp[0:NE, (A_LOG // 4) + ch * 512:(A_LOG // 4) + ch * 512 + 512],
                                    bank(b, 512, rows=NE))
                sV.inc(ins)
                brd(b, sV)

        @blk.vector
        def _(v):
            nbt = f32(A_TMP7 + 5760, 16)[:, 0:NE]
            vstep(v, v.memset(nbt, NEG))

        for tt in range(16):
            b = 2 + tt % 2

            @blk.tensor
            def _(t, tt=tt, b=b):
                sV.wait(t)
                wait_bank_free(t, b)
                ins = t.transpose(bank(b, NE, rows=128),
                                  AFp[0:NE, (A_LOG // 4) + tt * 128:(A_LOG // 4) + (tt + 1) * 128],
                                  ident_f[0:NE, 0:NE])
                sT.inc(ins)
                bwr(b, sT)

            @blk.vector
            def _(v, tt=tt, b=b):
                sT.wait(v)
                sS.wait(v)
                lt = f32(A_TMP7 + 4096, 16)[:, 0:NE]
                m1n = f32(A_TMP7 + 4224, 1)
                m2n = f32(A_TMP7 + 4352, 1)
                eq1 = f32(A_TMP7 + 4480, 16)[:, 0:NE]
                eq2 = f32(A_TMP7 + 4608, 16)[:, 0:NE]
                m1p = f32(A_TMP7 + 5376, 1)
                m2p = f32(A_TMP7 + 5504, 1)
                tq = f32(A_TMP7 + 5632, 16)[:, 0:NE]
                nbt = f32(A_TMP7 + 5760, 16)[:, 0:NE]
                vstep(v, v.tensor_copy(lt, bank(b, NE, rows=128)))
                brd(b, sV)
                vstep(v, v.reduce_max(m1n, lt, axis=AX, negate=True))
                vstep(v, v.tensor_scalar(m1p, m1n, -1.0, None, ALU.mult))
                vstep(v, v.tensor_scalar(eq1, lt, m1p, None, ALU.is_ge))
                vstep(v, v.tensor_mul(tq, eq1, nbt))
                vstep(v, v.tensor_add(tq, tq, lt))
                vstep(v, v.reduce_max(m2n, tq, axis=AX, negate=True))
                vstep(v, v.tensor_scalar(m2p, m2n, -1.0, None, ALU.mult))
                vstep(v, v.tensor_scalar(eq2, tq, m2p, None, ALU.is_ge))

            @blk.scalar
            def _(s, tt=tt):
                sV.wait(s)
                lt = f32(A_TMP7 + 4096, 16)[:, 0:NE]
                m1n = f32(A_TMP7 + 4224, 1)
                m2n = f32(A_TMP7 + 4352, 1)
                e2 = f32(A_TMP7 + 4864, 1)
                zz = f32(A_TMP7 + 4992, 1)
                es = f32(A_TMP7 + 5120, 16)[:, 0:NE]
                s.activation(out=es, in_=lt, func=AF.Exp, bias=m1n, accum_out=zz)
                ins = s.activation(out=e2, in_=m2n, func=AF.Exp, scale=-1.0, bias=m1n)
                sS.inc(ins)

            @blk.vector
            def _(v, tt=tt):
                sS.wait(v)
                e2 = f32(A_TMP7 + 4864, 1)
                zz = f32(A_TMP7 + 4992, 1)
                dd = f32(A_TMP7 + 5248, 1)
                eq1 = f32(A_TMP7 + 4480, 16)[:, 0:NE]
                eq2 = f32(A_TMP7 + 4608, 16)[:, 0:NE]
                vstep(v, v.scalar_tensor_tensor(dd, zz, 1e-9, e2, ALU.mult, ALU.add))
                vstep(v, v.tensor_scalar(dd, dd, 1.0, None, ALU.add))
                vstep(v, v.reciprocal(dd, dd))
                vstep(v, v.tensor_mul(e2, e2, dd))
                vstep(v, v.tensor_scalar(eq1, eq1, dd, None, ALU.mult))
                vstep(v, v.tensor_scalar(eq2, eq2, e2, None, ALU.mult))
                vstep(v, v.tensor_add(getok[:, tt * NE:(tt + 1) * NE], eq1, eq2))

        ln_stats(lambda ft, c, w: xn1[:, ft * T + c:ft * T + c + w], 8, RMS_EPS, False, A_TMP7 + 6144,
                 do_mu=False)
        ln_apply(lambda ft, c, w: xn1[:, ft * T + c:ft * T + c + w],
                 lambda ft, c, w: xhat[:, ft * T + c:ft * T + c + w],
                 8, A_TMP7 + 8192, sub_mu=False)

        @blk.vector
        def _(v):
            xb0 = bf(A_XNM, 8 * T)
            for ft in range(8):
                ins = v.tensor_copy(xb0[:, ft * T:(ft + 1) * T], xhat[:, ft * T:(ft + 1) * T])
            sV.inc(ins)

    def ph8():
        barrier()
        xres = f32(A_XRES8, 8 * T)
        tr = f32(A_TR8, 512)
        add_mo = stage == "l0" and _PH4 >= 4
        reload_xres(xres, A_TR8 + 4096, add_mo, False)

        for ft in range(8):
            for tg in range(4):
                b = tg % 2

                @blk.tensor
                def _(t, ft=ft, tg=tg, b=b):
                    sD.wait(t)
                    wait_bank_free(t, b)
                    for j in range(4):
                        tt = tg * 4 + j
                        ins = t.transpose(bank(b, 128, col0=j * 128),
                                          xres[:, ft * T + tt * 128:ft * T + (tt + 1) * 128],
                                          ident_f[:])
                    sT.inc(ins)
                    bwr(b, sT)

                @blk.vector
                def _(v, ft=ft, tg=tg, b=b):
                    sT.wait(v)
                    sD.wait(v)
                    ins = v.tensor_copy(tr, bank(b, 512))
                    sV.inc(ins)
                    brd(b, sV)

                @blk.sync
                def _(sy, ft=ft, tg=tg):
                    sV.wait(sy)
                    sy.dma_start(out=bass.AP(out_t.tensor, (tg * 4) * 128 * D + ft * 128,
                                             [[D, 128], [128 * D, 4], [1, 128]]),
                                 in_=tr).then_inc(sD.h, 16)
                    sD.n += 16

    tr_tok = {}

    def ph9():
        barrier()
        xhat = bf(A_XNM, 8 * T)
        s1 = bf(A_S1, 16 * 512)
        w1b = [bf(A_W1, 8 * HID), bf(A_W1 + 32 * KB, 8 * HID)]
        w2b = bf(A_W2, 16 * 1024)
        gtmps = [f32(A_TMP9, 512), f32(A_TMP9 + 2048, 512)]

        experts = [("r", e) for e in range(NE)] + [("s", 0), ("s", 1)]

        @blk.gpsimd
        def _(g):
            g.dma_start(out=w1b[0], in_=bass.AP(din["w1"].tensor, 0,
                                                [[HID, 128], [128 * HID, 8], [1, HID]])).then_inc(sG.h, 16)
            sG.n += 16

        for ei, (kind, e) in enumerate(experts):
            par = ei % 2
            @blk.vector
            def _(v, ei=ei, par=par, kind=kind, e=e):
                sG.wait(v)
                for kt in range(8):
                    c = (SM_RMS + kt * NE + e) if kind == "r" else (SM_RMSS + kt * 2 + e)
                    ins = v.tensor_scalar(w1b[par][:, kt * HID:(kt + 1) * HID],
                                          w1b[par][:, kt * HID:(kt + 1) * HID],
                                          smalls[:, c:c + 1], None, ALU.mult)
                sV.inc(ins)
            w1rdy = sV.n

            @blk.gpsimd
            def _(g, ei=ei, kind=kind, e=e):
                if ei > 0:
                    sT.wait(g, ot_tok.get(ei - 1, 0))
                if kind == "r":
                    src = bass.AP(din["w2"].tensor, e * HID * D,
                                  [[D, 128], [128 * D, 16], [1, D]])
                else:
                    src = bass.AP(din["w2s"].tensor, e * HID * D,
                                  [[D, 128], [128 * D, 16], [1, D]])
                g.dma_start(out=w2b, in_=src).then_inc(sW.h, 16)
                sW.n += 16
            w2rdy = sW.n

            for tch in range(4):
                tb0 = tch * 512
                for mt in range(16):
                    b = mt % 2

                    @blk.tensor
                    def _(t, ei=ei, par=par, tch=tch, mt=mt, b=b, w1rdy=w1rdy):
                        if tch == 0 and mt == 0:
                            sV.wait(t, w1rdy)
                            if ei == 0:
                                sV.wait(t)
                        wait_bank_free(t, b)
                        for kt in range(8):
                            ins = t.matmul(bank(b, 512), w1b[par][:, kt * HID + mt * 128:kt * HID + (mt + 1) * 128],
                                           xhat[:, kt * T + tb0:kt * T + tb0 + 512],
                                           start=(kt == 0), stop=(kt == 7))
                        sT.inc(ins)
                        bwr(b, sT)

                    @blk.scalar
                    def _(s, mt=mt, b=b):
                        sT.wait(s)
                        ins = s.activation(out=s1[:, mt * 512:(mt + 1) * 512], in_=bank(b, 512),
                                           func=AF.Silu)
                        sS.inc(ins)
                        brd(b, sS)

                for tt in range(4):
                    ttg = tch * 4 + tt
                    for nd in range(2):
                        b = 2 + (tt * 2 + nd) % 2

                        @blk.tensor
                        def _(t, ei=ei, tch=tch, tt=tt, nd=nd, b=b, w2rdy=w2rdy):
                            sS.wait(t)
                            sW.wait(t, w2rdy)
                            wait_bank_free(t, b)
                            for kt in range(16):
                                ins = t.matmul(bank(b, 512),
                                               s1[:, kt * 512 + tt * 128:kt * 512 + (tt + 1) * 128],
                                               w2b[:, kt * 1024 + nd * 512:kt * 1024 + (nd + 1) * 512],
                                               start=(kt == 0), stop=(kt == 15))
                            sT.inc(ins)
                            bwr(b, sT)
                        if tch == 3 and tt == 3 and nd == 1:
                            ot_tok[ei] = sT.n

                        aidx = ei * 32 + tch * 8 + tt * 2 + nd
                        asem = sA0 if aidx % 2 == 0 else sA1
                        gtmp = gtmps[aidx % 2]

                        @blk.vector
                        def _(v, ei=ei, kind=kind, e=e, ttg=ttg, nd=nd, b=b,
                              aidx=aidx, asem=asem, gtmp=gtmp):
                            sT.wait(v)
                            if moe_tok.get(aidx - 2) is not None:
                                asem.wait(v, moe_tok[aidx - 2])
                            if kind == "r":
                                gcol = getok[:, ttg * NE + e:ttg * NE + e + 1]
                                ins = v.tensor_scalar(gtmp, bank(b, 512), gcol, None, ALU.mult)
                            else:
                                ins = v.tensor_copy(gtmp, bank(b, 512))
                            sV.inc(ins)
                            brd(b, sV)

                        @blk.gpsimd
                        def _(g, ttg=ttg, nd=nd, aidx=aidx, asem=asem, gtmp=gtmp):
                            sV.wait(g)
                            g.dma_start(out=out_t[ttg * 128:(ttg + 1) * 128, nd * 512:(nd + 1) * 512],
                                        in_=gtmp, accum_op=ALU.add).then_inc(asem.h, 16)
                            asem.n += 16
                            moe_tok[aidx] = asem.n

                if tch == 3 and ei + 1 < len(experts):
                    nk, ne_ = experts[ei + 1]

                    @blk.gpsimd
                    def _(g, ei=ei, nk=nk, ne_=ne_, par=par):
                        sT.wait(g)
                        if nk == "r":
                            src = bass.AP(din["w1"].tensor, ne_ * D * HID,
                                          [[HID, 128], [128 * HID, 8], [1, HID]])
                        else:
                            src = bass.AP(din["w1s"].tensor, ne_ * D * HID,
                                          [[HID, 128], [128 * HID, 8], [1, HID]])
                        g.dma_start(out=w1b[1 - par], in_=src).then_inc(sG.h, 16)
                        sG.n += 16

    ot_tok = {}
    moe_tok = {}

    def ph_dbg1():
        barrier()
        xb = bf(A_XB, 16 * T)

        @blk.gpsimd
        def _(g):
            g.dma_start(out=dbg[:, 0:32768], in_=xb).then_inc(sG.h, 16)
            sG.n += 16
            g.dma_start(out=dbg[0:16, 32768:36864], in_=bmcm[:]).then_inc(sG.h, 16)
            sG.n += 16
            g.dma_start(out=dbg[:, 36864:37120], in_=aneg_t[:]).then_inc(sG.h, 16)
            sG.n += 16
        barrier()

    run2 = stage != "ln0"
    run3 = stage not in ("ln0", "ffn")
    run4 = run3 and stage not in ("pre", "dbg1")
    run5 = stage in ("dbg4", "dbg5", "attn", "dbg6", "router", "dbg3", "noattn", "moec", "full")
    run6 = stage in ("attn", "dbg6", "full")
    run7 = stage in ("router", "dbg3", "noattn", "full")
    run9 = stage in ("noattn", "moec", "full")

    ph1()
    if run2:
        ph2()
    if run3:
        ph3()
    if stage == "dbg1":
        ph_dbg1()
    if run4:
        ph4()
    if run5:
        ph5()
    if run6:
        ph6()
    if run7:
        ph7()
    if stage == "dbg4":
        barrier()

        @blk.gpsimd
        def _(g):
            g.dma_start(out=dbg[:, 4096:4096 + 16384], in_=bf(A_XNM, 8 * T)).then_inc(sG.h, 16)
            g.dma_start(out=dbg[:, 20480:20480 + 16384], in_=f32(A_XRES1, 8 * T)).then_inc(sG.h, 16)
            sG.n += 32
        barrier()
    if stage == "dbg6":
        barrier()

        @blk.gpsimd
        def _(g):
            g.dma_start(out=dbg[:, 0:16384], in_=bf(A_OH, 8 * T)).then_inc(sG.h, 16)
            g.dma_start(out=dbg[:, 16384:18432], in_=bf(A_QP, T)).then_inc(sG.h, 16)
            g.dma_start(out=dbg[:, 18432:20480], in_=bf(A_KP, T)).then_inc(sG.h, 16)
            g.dma_start(out=dbg[:, 20480:23040], in_=bf(A_EMB, POSW)).then_inc(sG.h, 16)
            g.dma_start(out=dbg[:, 23040:23041], in_=f32(A_SP6 + 64, 1),
                        allow_slow_non_contiguous=True).then_inc(sG.h, 16)
            g.dma_start(out=dbg[:, 23041:23042], in_=f32(A_SP6 + 128, 1),
                        allow_slow_non_contiguous=True).then_inc(sG.h, 16)
            g.dma_start(out=dbg[:, 23042:23050], in_=f32(A_SP6, 8)).then_inc(sG.h, 16)
            g.dma_start(out=dbg[:, 23050:25098], in_=bf(A_VP, 16 * 128)).then_inc(sG.h, 16)
            g.dma_start(out=dbg[:, 24576:40960], in_=bf(A_XNM, 8 * T)).then_inc(sG.h, 16)
            sG.n += 144
        barrier()

    if stage == "dbg3":
        barrier()

        @blk.gpsimd
        def _(g):
            g.dma_start(out=dbg[:, 0:224], in_=getok[:]).then_inc(sG.h, 16)
            g.dma_start(out=dbg[0:NE, 224:2272], in_=f32(A_LOG, 2048)[0:NE, :]).then_inc(sG.h, 16)
            g.dma_start(out=dbg[:, 4096:4096 + 16384], in_=bf(A_XNM, 8 * T)).then_inc(sG.h, 16)
            sG.n += 48
        barrier()
    ph8()
    if run9:
        ph9()

    @blk.sync
    def _(sy):
        sy.wait_ge(sD.h, sD.n)
        sy.wait_ge(sG.h, sG.n)
        sy.wait_ge(sP.h, sP.n)
        sy.wait_ge(sW.h, sW.n)
        sy.wait_ge(sA0.h, sA0.n)
        sy.wait_ge(sA1.h, sA1.n)

    stk.close()
    return nc


def kernel(**inputs):
    nc = build(_STAGE)
    in_maps = [stage_inputs(inputs, c) for c in range(8)]
    res = run_bass_kernel_spmd(nc, in_maps, core_ids=list(range(8)))
    out = np.stack([res.results[c]["out_t"] for c in range(8)], axis=0)
    return out.astype(np.float32)
